# version 04
# baseline (speedup 1.0000x reference)
import numpy as np
import concourse.bass as bass
import concourse.mybir as mybir
from concourse.bass_utils import run_bass_kernel_spmd

F32 = mybir.dt.float32
I32 = mybir.dt.int32
ALU = mybir.AluOpType
ACT = mybir.ActivationFunctionType
AX = mybir.AxisListType

D = 1024
DI = 2048
DTR = 64
NS = 16
NMY = 8
TH = 4096
TF = 8192
FF = 4096
NE = 4
NR = 4

T1X = 256
N1X = TH // T1X
T1Z = 512
N1Z = TH // T1Z
T2 = 512
N2 = TF // T2
T3 = 512
N3 = TH // T3

NDC = DI // 128
NDM = D // 128
NFC = FF // 128

ENGS = ("sync", "scalar", "vector", "tensor", "gpsimd")
PAIRS = [[0, 1], [2, 3], [4, 5], [6, 7]]


class Build:
    def __init__(self, nc, semnames):
        self.nc = nc
        self.q = {e: [] for e in ENGS}
        self.cnt = {s: 0 for s in semnames}
        self.sems = {}
        self.regs = {}

    def emit(self, eng, fn, waits=(), inc=None, amt=1):
        ws = tuple(w for w in waits if w is not None)
        self.q[eng].append((fn, ws, inc, amt))
        if inc is not None:
            self.cnt[inc] += amt
            return (inc, self.cnt[inc])
        return None

    def dma(self, eng, out, in_, waits=(), sem=None):
        sem = sem or ("GD" if eng == "gpsimd" else "LD")

        def fn(e, out=out, in_=in_):
            return e.dma_start(out=out, in_=in_)
        return self.emit(eng, fn, waits, sem, 16)

    def dma_if(self, eng, out0, in0, out1, in1, waits=(), sem=None):
        sem = sem or ("GD" if eng == "gpsimd" else "LD")
        semn, regs, sems = sem, self.regs, self.sems

        def fn(e, out0=out0, in0=in0, out1=out1, in1=in1):
            with e.If_eq(regs[eng], 0):
                e.dma_start(out=out0, in_=in0).then_inc(sems[semn], 16)
            with e.Else():
                e.dma_start(out=out1, in_=in1).then_inc(sems[semn], 16)
            return None
        return self.emit(eng, fn, waits, semn, 16)

    def tok(self, sem):
        return (sem, self.cnt[sem])


def build_program(debug=False):
    nc = bass.Bass("TRN2", target_bir_lowering=False)

    def din(name, shape, dt=F32):
        return nc.dram_tensor(name, list(shape), dt, kind="ExternalInput").ap()

    u_d = din("u", [D, TH])
    par_d = din("par", [1, 1], I32)
    winx_d = din("winx", [D, DI])
    winz_d = din("winz", [D, DI])
    binx_d = din("binx", [128, NDC])
    binz_d = din("binz", [128, NDC])
    wx_d = din("wx", [DI, 96])
    bx_d = din("bx", [96, 1])
    wdt_d = din("wdt", [DTR, DI])
    bdt_d = din("bdt", [128, NDC])
    amy_d = din("amy", [128, NDC * NMY])
    dvec_d = din("dvec", [128, NDC])
    sel_d = din("sel", [32, 2 * NMY * 128])
    wout_d = din("wout", [DI, D])
    bout_d = din("bout", [128, NDM])
    wg_d = din("wg", [D, NE])
    bg_d = din("bg", [1, NE])
    wi_d = din("wi", [D, FF])
    acat_d = din("acat", [D, NE * NR])
    bcat_d = din("bcat", [16, NFC * NE * 128])
    wo_d = din("wo", [FF, D])

    out_d = nc.dram_tensor("out", [D, TH], F32, kind="ExternalOutput").ap()
    out_v = out_d.rearrange("(c p) t -> p c t", p=128)

    def shr(name, shape):
        return nc.dram_tensor(name, list(shape), F32, kind="Internal",
                              addr_space="Shared").ap()

    def pri(name, shape):
        return nc.dram_tensor(name, list(shape), F32, kind="Internal").ap()

    sh_delta = shr("sh_delta", [2, NDC, 128, TH])
    sh_deltax = shr("sh_deltax", [2, NDC, 128, TH])
    sh_bc = shr("sh_bc", [2, 32, TH])
    sh_yp = shr("sh_yp", [2, NDC, 128, TH])
    p_yp = pri("p_yp", [NDC, 128, TH])
    p_s = pri("p_s", [NDC, 128, TH])
    p_g1 = pri("p_g1", [NDC, 128, TH])
    p_rs = pri("p_rs", [1, 2 * TH])
    p_xr2 = pri("p_xr2", [NDM, 128, TH])
    p_xh2 = pri("p_xh2", [NDM, 128, TH])
    p_dy = pri("p_dy", [NDC, 128, TH])
    p_dg = pri("p_dg", [NFC, 128, TH])
    p_dw = pri("p_dw", [1, NE * TH])
    p_dll = pri("p_dll", [16, TH])
    p_dbase = pri("p_dbase", [NE, 128, TH])
    p_dyg = pri("p_dyg", [NDC, 128, TH])
    bar_i = pri("bar_i", [1, 1])
    bar_o = pri("bar_o", [1, 1])

    ARENA_COLS = 52000
    arena = nc.alloc_sbuf_tensor("arena", [128, ARENA_COLS], F32).ap()
    par_sb = nc.alloc_sbuf_tensor("par_sb", [1, 1], I32).ap()

    class Alloc:
        def __init__(self, off=0):
            self.off = off

        def a(self, cols):
            assert self.off + cols <= ARENA_COLS, (self.off, cols)
            v = arena[:, self.off:self.off + cols]
            self.off += cols
            return v

    AL = Alloc()
    ones_col = AL.a(1)
    ones_row = AL.a(128)
    binx_sb = AL.a(NDC)
    binz_sb = AL.a(NDC)
    bx_sb = AL.a(1)
    bdt_sb = AL.a(NDC)
    amy_sb = AL.a(NDC * NMY)
    dvec_sb = AL.a(NDC)
    bout_sb = AL.a(NDM)
    bg_sb = AL.a(NE)
    eps_sb = AL.a(1)
    car_sb = AL.a(NDC * NMY)
    PERM = AL.off

    A = Alloc(PERM)
    winx_sb = A.a(8 * DI)
    wx_sb = A.a(NDC * 96)
    wdt_sb = A.a(DI)
    u1 = [A.a(8 * T1X) for _ in range(2)]
    usq1 = A.a(8 * T1X)
    xh1 = A.a(8 * T1X)
    xsb = A.a(NDC * T1X)
    delt1 = A.a(NDC * T1X)
    dtmp1 = A.a(T1X)
    lntmp = A.a(3 * T1X)
    g11 = A.a(NDC * T1X)
    smreg = A.a(2 * TH)
    assert A.off <= ARENA_COLS, A.off
    rsacc = smreg[0:1, :]
    bcacc = smreg[32:64, 0:TH]

    A = Alloc(PERM)
    winz_sb = A.a(8 * DI)
    u2 = [A.a(8 * T1Z) for _ in range(2)]
    xh1z = A.a(8 * T1Z)
    rs2 = A.a(2 * T1Z)
    ssb = A.a(NDC * T1Z)
    assert A.off <= ARENA_COLS

    A = Alloc(PERM)
    sel_sb = A.a(2 * NMY * 128)
    bct = A.a(T2)
    dlt2 = [A.a(T2) for _ in range(4)]
    dxt2 = [A.a(T2) for _ in range(4)]
    brep = A.a(NMY * T2)
    crep = A.a(NMY * T2)
    da2 = [A.a(NMY * T2) for _ in range(2)]
    b2 = A.a(NMY * T2)
    h2 = A.a(NMY * T2)
    hc2 = A.a(NMY * T2)
    y2 = A.a(NDC * T2)
    assert A.off <= ARENA_COLS

    A = Alloc(PERM)
    wout_sb = A.a(NDC * D)
    big1 = A.a(NDC * T3)
    big2 = A.a(NDC * T3)
    yg3 = A.a(NDC * T3)
    sq3a = A.a(2 * T3)
    rp3 = A.a(4 * T3)
    assert A.off <= ARENA_COLS
    xh3 = big1[:, 0:NDM * T3]
    u3 = big1[:, NDM * T3:2 * NDM * T3]
    xr3 = big2[:, 0:NDM * T3]

    A = Alloc(PERM)
    wg_sb = A.a(8 * NE)
    acat_sb = A.a(8 * NE * NR)
    bcatc = [A.a(NE * 128) for _ in range(2)]
    xh4 = [A.a(NDM * T3) for _ in range(2)]
    xr4 = A.a(NDM * T3)
    wic = [A.a(8 * 128) for _ in range(2)]
    woc = [A.a(D) for _ in range(2)]
    llsb16 = A.a(T3)
    lgt4 = A.a(T3)
    row0 = A.a(3 * NE * T3)
    wrep = A.a(NE * T3)
    lor4 = A.a(NE * T3)
    hpr = A.a(2 * NE * T3)
    hp4 = hpr[:, 0:NE * T3]
    hr4 = hpr[:, NE * T3:2 * NE * T3]
    out4 = hpr
    hw4 = A.a(NE * T3)
    g4 = A.a(NFC * T3)
    assert A.off <= ARENA_COLS, A.off
    expl0 = row0[0:1, 0:NE * T3]
    w40 = row0[0:1, NE * T3:2 * NE * T3]
    ts0 = row0[0:1, 2 * NE * T3:3 * NE * T3]
    lgr0 = ts0

    ps = nc.alloc_psum_tensor("ps", [128, 4096], F32).ap()

    def bank(i, cols=512):
        return ps[:, i * 512:i * 512 + cols]

    SEMS = ("LD", "GD", "T", "S", "V", "BAR")
    b = Build(nc, SEMS)

    def r2(ap, c, t):
        return ap.rearrange("p (c t) -> p c t", c=c)

    def cpt(dram3):
        return dram3.rearrange("c p t -> p c t")

    def vemit(fn, waits=()):
        return b.emit("vector", fn, waits, "V")

    def semit(fn, waits=()):
        return b.emit("scalar", fn, waits, "S")

    def temit(fn, waits=()):
        return b.emit("tensor", fn, waits, "T")

    tok_par = b.dma("sync", par_sb[:, :], par_d[:, :])
    b.dma("sync", binx_sb[:, :], binx_d[:, :])
    b.dma("sync", binz_sb[:, :], binz_d[:, :])
    b.dma("sync", bx_sb[0:96, :], bx_d[:, :])
    b.dma("sync", bdt_sb[:, :], bdt_d[:, :])
    b.dma("sync", amy_sb[:, :], amy_d[:, :])
    b.dma("sync", dvec_sb[:, :], dvec_d[:, :])
    b.dma("sync", bout_sb[:, :], bout_d[:, :])
    b.dma("sync", bg_sb[0:1, :], bg_d[:, :])
    tok_consts = b.tok("LD")

    vemit(lambda e: e.memset(ones_col[:, :], 1.0), waits=[tok_consts])
    vemit(lambda e: e.memset(ones_row[0:1, :], 1.0))
    vemit(lambda e: e.memset(car_sb[:, :], 0.0))
    vemit(lambda e: e.memset(eps_sb[:, :], 1e-5))
    tok_ones = b.tok("V")
    tok_aexp = semit(lambda e: e.activation(amy_sb[:, :], amy_sb[:, :], ACT.Exp),
                     waits=[tok_consts])
    tok_amy = vemit(lambda e: e.tensor_scalar_mul(amy_sb[:, :], amy_sb[:, :], -1.0),
                    waits=[tok_aexp])

    for eng in ("sync", "gpsimd"):
        def fn(e, eng=eng):
            e.reg_load(b.regs[eng], par_sb[0:1, 0:1])
            return None
        b.emit(eng, fn, waits=[tok_par], inc=None)

    u_v = u_d.rearrange("(c p) t -> p c t", p=128)

    def join_all():
        return [b.tok("T"), b.tok("S"), b.tok("V"), b.tok("GD")]

    b.dma("sync", r2(winx_sb, 8, DI), winx_d.rearrange("(k p) f -> p k f", p=128))
    b.dma("sync", r2(wx_sb, NDC, 96), wx_d.rearrange("(k p) f -> p k f", p=128))
    b.dma("sync", wdt_sb[0:DTR, :].rearrange("p (c f) -> p c f", c=NDC),
          wdt_d.rearrange("k (c f) -> k c f", c=NDC))
    tok_w1x = b.tok("LD")

    u1_tok = [None, None]
    u1_free = [None, None]

    def ph1x_load(k):
        slot = k % 2
        u1_tok[slot] = b.dma("sync", r2(u1[slot], 8, T1X),
                             u_v[:, :, k * T1X:(k + 1) * T1X],
                             waits=[u1_free[slot]])

    ph1x_load(0)
    jtok = []
    spill_d = spill_dx = spill_g1 = None

    for k in range(N1X):
        slot = k % 2
        if k + 1 < N1X:
            ph1x_load(k + 1)
        ut = r2(u1[slot], 8, T1X)
        base_w = [u1_tok[slot], tok_w1x, tok_ones, tok_amy] + jtok

        tok_sq = semit(lambda e, slot=slot: e.activation(usq1[:, :], u1[slot][:, :],
                                                         ACT.Square), waits=base_w)
        for c in range(8):
            temit(lambda e, c=c, ut=ut: e.matmul(bank(2)[0:1, 0:T1X], ones_col[:, :],
                                                 ut[:, c, :], start=(c == 0),
                                                 stop=(c == 7)),
                  waits=base_w if c == 0 else ())
            temit(lambda e, c=c: e.matmul(bank(5)[0:1, 0:T1X], ones_col[:, :],
                                          usq1[:, c * T1X:(c + 1) * T1X],
                                          start=(c == 0), stop=(c == 7)),
                  waits=[tok_sq] if c == 0 else ())
        tok_stats = b.tok("T")
        rsl_r = rsacc[0:1, k * T1X:(k + 1) * T1X]
        rsl_m = rsacc[0:1, TH + k * T1X:TH + (k + 1) * T1X]
        lm = lntmp[0:1, 0:T1X]
        lv = lntmp[0:1, T1X:2 * T1X]
        lq = lntmp[0:1, 2 * T1X:3 * T1X]
        vemit(lambda e, lm=lm: e.tensor_scalar_mul(lm, bank(2)[0:1, 0:T1X],
                                                   1.0 / D), waits=[tok_stats])
        vemit(lambda e, lv=lv: e.tensor_scalar_mul(lv, bank(5)[0:1, 0:T1X],
                                                   1.0 / D))
        vemit(lambda e, lm=lm, lq=lq: e.tensor_mul(lq, lm, lm))
        tok_var = vemit(lambda e, lv=lv, lq=lq: e.tensor_sub(lv, lv, lq))
        semit(lambda e, lv=lv: e.activation(lv, lv, ACT.Ln,
                                            bias=eps_sb[0:1, :]), waits=[tok_var])
        tok_rstd = semit(lambda e, rsl_r=rsl_r, lv=lv: e.activation(
            rsl_r, lv, ACT.Exp, scale=-0.5))
        tok_mr = vemit(lambda e, rsl_r=rsl_r, rsl_m=rsl_m, lm=lm: e.tensor_mul(
            rsl_m, lm, rsl_r), waits=[tok_rstd])
        temit(lambda e, rsl_r=rsl_r: e.matmul(bank(3, T1X), ones_row[0:1, :], rsl_r,
                                              start=True, stop=True), waits=[tok_mr])
        tok_rep = temit(lambda e, rsl_m=rsl_m: e.matmul(bank(4, T1X), ones_row[0:1, :],
                                                        rsl_m, start=True, stop=True))
        for c in range(8):
            vemit(lambda e, c=c, ut=ut: e.tensor_mul(xh1[:, c * T1X:(c + 1) * T1X],
                                                     ut[:, c, :], bank(3, T1X)),
                  waits=[tok_rep] if c == 0 else ())
            vemit(lambda e, c=c: e.tensor_sub(xh1[:, c * T1X:(c + 1) * T1X],
                                              xh1[:, c * T1X:(c + 1) * T1X],
                                              bank(4, T1X)))
        tok_xh = b.tok("V")
        u1_free[slot] = tok_xh

        xw = r2(winx_sb, 8, DI)
        ev = {}
        for oc in range(NDC):
            bk = oc % 2
            w = [tok_xh, ev.get(oc - 2)]
            for kk in range(8):
                temit(lambda e, oc=oc, kk=kk, bk=bk: e.matmul(
                    bank(bk, T1X), xw[:, kk, oc * 128:(oc + 1) * 128],
                    xh1[:, kk * T1X:(kk + 1) * T1X],
                    start=(kk == 0), stop=(kk == 7)),
                    waits=w if kk == 0 else ())
            tmm = b.tok("T")
            semit(lambda e, oc=oc, bk=bk: e.activation(
                xsb[:, oc * T1X:(oc + 1) * T1X], bank(bk, T1X), ACT.Identity,
                bias=binx_sb[:, oc:oc + 1]), waits=[tmm])
            semit(lambda e, oc=oc: e.activation(
                g11[:, oc * T1X:(oc + 1) * T1X], xsb[:, oc * T1X:(oc + 1) * T1X],
                ACT.Copy, scale=dvec_sb[:, oc:oc + 1]))
            ev[oc] = b.tok("S")
        tok_xall = b.tok("S")

        wxv = r2(wx_sb, NDC, 96)
        for kk in range(NDC):
            temit(lambda e, kk=kk: e.matmul(
                bank(6, T1X)[0:96, :], wxv[:, kk, :],
                xsb[:, kk * T1X:(kk + 1) * T1X],
                start=(kk == 0), stop=(kk == NDC - 1)),
                waits=[tok_xall] if kk == 0 else ())
        tok_wx = b.tok("T")
        tok_dbc = semit(lambda e: e.activation(usq1[0:96, 0:T1X], bank(6, T1X)[0:96, :],
                                               ACT.Identity, bias=bx_sb[0:96, :]),
                        waits=[tok_wx])
        tok_bccp = vemit(lambda e, k=k: e.tensor_copy(
            bcacc[:, k * T1X:(k + 1) * T1X], usq1[64:96, 0:T1X]), waits=[tok_dbc])

        wdv = wdt_sb[0:DTR, :].rearrange("p (c f) -> p c f", c=NDC)
        dev = {}
        for oc in range(NDC):
            bk = oc % 2
            w = [tok_dbc, dev.get(oc - 2)]
            if oc < 2:
                w.append(ev[NDC - 2 + oc])
            temit(lambda e, oc=oc, bk=bk: e.matmul(
                bank(bk, T1X), wdv[:, oc, :], usq1[0:DTR, 0:T1X],
                start=True, stop=True), waits=w)
            tmm = b.tok("T")
            semit(lambda e, oc=oc, bk=bk: e.activation(
                dtmp1[:, :], bank(bk, T1X), ACT.Exp, bias=bdt_sb[:, oc:oc + 1]),
                waits=[tmm])
            semit(lambda e, oc=oc: e.activation(
                delt1[:, oc * T1X:(oc + 1) * T1X], dtmp1[:, :], ACT.Ln, bias=1.0))
            dev[oc] = b.tok("S")
        tok_delta = b.tok("S")
        tok_tend = b.tok("T")

        tok_dx = vemit(lambda e: e.tensor_mul(xsb[:, :], delt1[:, :], xsb[:, :]),
                       waits=[tok_delta, tok_xall])

        spill_d = b.dma_if(
            "gpsimd",
            cpt(sh_delta[0, :, :, k * T1X:(k + 1) * T1X]), r2(delt1, NDC, T1X),
            cpt(sh_delta[1, :, :, k * T1X:(k + 1) * T1X]), r2(delt1, NDC, T1X),
            waits=[tok_delta, spill_d])
        spill_dx = b.dma_if(
            "gpsimd",
            cpt(sh_deltax[0, :, :, k * T1X:(k + 1) * T1X]), r2(xsb, NDC, T1X),
            cpt(sh_deltax[1, :, :, k * T1X:(k + 1) * T1X]), r2(xsb, NDC, T1X),
            waits=[tok_dx, spill_dx])
        spill_g1 = b.dma("gpsimd", cpt(p_g1[:, :, k * T1X:(k + 1) * T1X]),
                         r2(g11, NDC, T1X), waits=[tok_xall, spill_g1])
        jtok = [spill_d, spill_dx, spill_g1, tok_dx, tok_bccp, tok_delta, tok_tend]

    b.dma_if("gpsimd", sh_bc[0, :, :], bcacc[:, :],
             sh_bc[1, :, :], bcacc[:, :], waits=[b.tok("V"), b.tok("S")])
    b.dma("gpsimd", p_rs[:, :], rsacc[:, :])
    ph1x_done = join_all()

    tok_wz = b.dma("sync", r2(winz_sb, 8, DI),
                   winz_d.rearrange("(k p) f -> p k f", p=128), waits=ph1x_done)
    u2_tok = [None, None]
    u2_free = [None, None]

    def ph1z_load(k):
        slot = k % 2
        u2_tok[slot] = b.dma("sync", r2(u2[slot], 8, T1Z),
                             u_v[:, :, k * T1Z:(k + 1) * T1Z],
                             waits=[u2_free[slot], tok_wz])

    ph1z_load(0)
    rs_tok = b.dma("sync",
                   rs2[0:1, :].rearrange("o (a t) -> o a t", a=2),
                   p_rs[0:1, :].rearrange("o (a t) -> o a t", a=2)[:, :, 0:T1Z],
                   waits=ph1x_done)
    rs_free = None
    spill_s = None
    mm_free = None

    for k in range(N1Z):
        slot = k % 2
        if k + 1 < N1Z:
            ph1z_load(k + 1)
        ut = r2(u2[slot], 8, T1Z)
        w = [u2_tok[slot], rs_tok, rs_free] + (ph1x_done if k == 0 else [])
        temit(lambda e: e.matmul(bank(3, T1Z), ones_row[0:1, :], rs2[0:1, 0:T1Z],
                                 start=True, stop=True), waits=w)
        tok_rep = temit(lambda e: e.matmul(bank(4, T1Z), ones_row[0:1, :],
                                           rs2[0:1, T1Z:2 * T1Z],
                                           start=True, stop=True))
        for c in range(8):
            vemit(lambda e, c=c, ut=ut: e.tensor_mul(xh1z[:, c * T1Z:(c + 1) * T1Z],
                                                     ut[:, c, :], bank(3, T1Z)),
                  waits=[tok_rep] if c == 0 else ())
            vemit(lambda e, c=c: e.tensor_sub(xh1z[:, c * T1Z:(c + 1) * T1Z],
                                              xh1z[:, c * T1Z:(c + 1) * T1Z],
                                              bank(4, T1Z)))
        tok_xh = b.tok("V")
        u2_free[slot] = tok_xh
        rs_free = tok_xh
        if k + 1 < N1Z:
            rs_tok = b.dma("sync",
                           rs2[0:1, :].rearrange("o (a t) -> o a t", a=2),
                           p_rs[0:1, :].rearrange("o (a t) -> o a t", a=2)
                           [:, :, (k + 1) * T1Z:(k + 2) * T1Z],
                           waits=[tok_rep])

        zw = r2(winz_sb, 8, DI)
        ev = {}
        for oc in range(NDC):
            bk = oc % 2
            w = [tok_xh, ev.get(oc - 2)]
            if oc < 2:
                w.append(mm_free)
            for kk in range(8):
                temit(lambda e, oc=oc, kk=kk, bk=bk: e.matmul(
                    bank(bk, T1Z), zw[:, kk, oc * 128:(oc + 1) * 128],
                    xh1z[:, kk * T1Z:(kk + 1) * T1Z],
                    start=(kk == 0), stop=(kk == 7)),
                    waits=w if kk == 0 else ())
            tmm = b.tok("T")
            semit(lambda e, oc=oc, bk=bk: e.activation(
                ssb[:, oc * T1Z:(oc + 1) * T1Z], bank(bk, T1Z), ACT.Silu,
                bias=binz_sb[:, oc:oc + 1]), waits=[tmm])
            ev[oc] = b.tok("S")
        mm_free = b.tok("S")
        spill_s = b.dma("gpsimd", cpt(p_s[:, :, k * T1Z:(k + 1) * T1Z]),
                        r2(ssb, NDC, T1Z), waits=[mm_free, spill_s])

    def barrier(waits):
        def fn(e):
            return e.collective_compute("AllReduce", ALU.add, replica_groups=PAIRS,
                                        ins=[bar_i[:, :]], outs=[bar_o[:, :]])
        return b.emit("gpsimd", fn, waits=waits, inc="BAR")

    tok_bar1 = barrier(join_all())

    pj = join_all() + [tok_bar1]
    tok_sel = b.dma("sync", sel_sb[0:32, :], sel_d[:, :], waits=pj)
    selv = sel_sb[0:32, :].rearrange("p (s f) -> p s f", s=2 * NMY)

    dl_tok = {}
    dl_free = {}
    next_load = [0]

    def ph2_issue_loads(upto):
        while next_load[0] <= min(upto, N2 * NDC - 1):
            gi = next_load[0]
            next_load[0] += 1
            jj, dcc = divmod(gi, NDC)
            slot = gi % 4
            half, tof = jj // 8, (jj % 8) * T2
            w = [tok_bar1]
            if slot in dl_free:
                w.extend(dl_free[slot])
            t1 = b.dma("sync", dlt2[slot][:, :], sh_delta[half, dcc, :, tof:tof + T2],
                       waits=w)
            t2 = b.dma("sync", dxt2[slot][:, :], sh_deltax[half, dcc, :, tof:tof + T2])
            dl_tok[gi] = (t1, t2)

    ph2_issue_loads(3)

    bct_tok = None
    bct_free = None
    da_slot = [0]
    da_free = [None, None]
    spill_y = None
    tile_v_end = None

    for j in range(N2):
        half, tof = j // 8, (j % 8) * T2
        bct_tok = b.dma("sync", bct[0:32, :], sh_bc[half, :, tof:tof + T2],
                        waits=[tok_bar1, bct_free])
        evs = []
        for n in range(NMY):
            for which, dst in ((0, brep), (1, crep)):
                bk = (2 * n + which) % 2
                w = [tok_sel, bct_tok]
                if len(evs) >= 2:
                    w.append(evs[-2])
                if len(evs) < 2 and tile_v_end is not None:
                    w.append(tile_v_end)
                soff = which * NMY
                temit(lambda e, n=n, soff=soff, bk=bk: e.matmul(
                    bank(bk, T2), selv[:, soff + n, :], bct[0:32, :],
                    start=True, stop=True), waits=w)
                tmm = b.tok("T")
                semit(lambda e, n=n, dst=dst, bk=bk: e.activation(
                    dst[:, n * T2:(n + 1) * T2], bank(bk, T2), ACT.Copy),
                    waits=[tmm])
                evs.append(b.tok("S"))
        tok_bcr = b.tok("S")
        bct_free = b.tok("T")

        for dc in range(NDC):
            gi = j * NDC + dc
            ph2_issue_loads(gi + 3)
            slot = gi % 4
            ltok = dl_tok[gi]

            das = da_slot[0]
            da_slot[0] = 1 - da_slot[0]
            dav = da2[das]
            for n in range(NMY):
                w = [ltok[0], tok_amy]
                if n == 0:
                    w.append(da_free[das])
                    if dc == 0:
                        w.append(tok_bcr)
                semit(lambda e, n=n, dc=dc, dav=dav, slot=slot: e.activation(
                    dav[:, n * T2:(n + 1) * T2], dlt2[slot][:, :], ACT.Exp,
                    scale=amy_sb[:, dc * NMY + n:dc * NMY + n + 1]),
                    waits=w if n == 0 else ())
            tok_da = b.tok("S")

            tok_b = vemit(lambda e, slot=slot: e.scalar_tensor_tensor(
                r2(b2, NMY, T2),
                dxt2[slot][:, :].rearrange("p (a t) -> p a t", a=1)
                .broadcast_to((128, NMY, T2)),
                1.0, r2(brep, NMY, T2), ALU.mult, ALU.mult),
                waits=[ltok[1], tok_bcr, tok_da])
            dl_free[slot] = (tok_da, tok_b)

            for n in range(NMY):
                vemit(lambda e, n=n, dav=dav, dc=dc: e.tensor_tensor_scan(
                    h2[:, n * T2:(n + 1) * T2], dav[:, n * T2:(n + 1) * T2],
                    b2[:, n * T2:(n + 1) * T2],
                    car_sb[:, dc * NMY + n:dc * NMY + n + 1],
                    ALU.mult, ALU.add))
            for n in range(NMY):
                vemit(lambda e, n=n, dc=dc: e.tensor_copy(
                    car_sb[:, dc * NMY + n:dc * NMY + n + 1],
                    h2[:, n * T2 + T2 - 1:n * T2 + T2]))
            da_free[das] = b.tok("V")
            vemit(lambda e: e.tensor_tensor(
                hc2[:, :].rearrange("p (t n) -> p n t", n=NMY),
                r2(h2, NMY, T2), r2(crep, NMY, T2), ALU.mult))
            w = [spill_y] if (dc == 0 and spill_y is not None) else ()
            vemit(lambda e, dc=dc: e.tensor_reduce(
                y2[:, dc * T2:(dc + 1) * T2],
                hc2[:, :].rearrange("p (t n) -> p t n", n=NMY), AX.X, ALU.add),
                waits=w)
        tile_v_end = b.tok("V")

        def fny(e, j=j, half=half, tof=tof):
            with e.If_eq(b.regs["gpsimd"], half):
                e.dma_start(out=cpt(p_yp[:, :, tof:tof + T2]),
                            in_=r2(y2, NDC, T2)).then_inc(b.sems["GD"], 16)
            with e.Else():
                e.dma_start(out=cpt(sh_yp[half, :, :, tof:tof + T2]),
                            in_=r2(y2, NDC, T2)).then_inc(b.sems["GD"], 16)
            return None
        spill_y = b.emit("gpsimd", fny, waits=[tile_v_end, spill_y], inc="GD", amt=16)

    tok_bar2 = barrier(join_all())

    pj = join_all() + [tok_bar2]
    tok_wo3 = b.dma("sync", r2(wout_sb, NDC, D),
                    wout_d.rearrange("(k p) f -> p k f", p=128), waits=pj)
    fr = {}
    for i in range(N3):
        tof = i * T3
        tok_yp1 = b.dma("sync", r2(big1, NDC, T3), cpt(p_yp[:, :, tof:tof + T3]),
                        waits=[tok_bar2, fr.get("big1")])

        def fnyp(e, tof=tof):
            with e.If_eq(b.regs["sync"], 0):
                e.dma_start(out=r2(big2, NDC, T3),
                            in_=cpt(sh_yp[0, :, :, tof:tof + T3])).then_inc(
                                b.sems["LD"], 16)
            with e.Else():
                e.dma_start(out=r2(big2, NDC, T3),
                            in_=cpt(sh_yp[1, :, :, tof:tof + T3])).then_inc(
                                b.sems["LD"], 16)
            return None
        tok_yp2 = b.emit("sync", fnyp, waits=[tok_bar2, fr.get("big2")],
                         inc="LD", amt=16)
        tok_y = vemit(lambda e: e.tensor_add(yg3[:, :], big1[:, :], big2[:, :]),
                      waits=[tok_yp1, tok_yp2, fr.get("yg")])
        tok_g1 = b.dma("sync", r2(big2, NDC, T3), cpt(p_g1[:, :, tof:tof + T3]),
                       waits=[tok_y])
        tok_yg1 = vemit(lambda e: e.tensor_add(yg3[:, :], yg3[:, :], big2[:, :]),
                        waits=[tok_g1])
        tok_s = b.dma("sync", r2(big2, NDC, T3), cpt(p_s[:, :, tof:tof + T3]),
                      waits=[tok_yg1])
        tok_yg = vemit(lambda e: e.tensor_mul(yg3[:, :], yg3[:, :], big2[:, :]),
                       waits=[tok_s])
        fr["big2"] = tok_yg
        if debug:
            b.dma("gpsimd", cpt(p_dy[:, :, tof:tof + T3]), r2(big2, NDC, T3),
                  waits=[tok_yg])
            b.dma("gpsimd", cpt(p_dyg[:, :, tof:tof + T3]), r2(yg3, NDC, T3),
                  waits=[tok_yg])
        fr["big1"] = tok_y
        tok_u = b.dma("sync", r2(u3, NDM, T3), u_v[:, :, tof:tof + T3],
                      waits=[tok_y, fr.get("xh3")])

        wov = r2(wout_sb, NDC, D)
        xev = {}
        for do in range(NDM):
            bk = do % 2
            w = [tok_yg, tok_wo3, xev.get(do - 2)]
            if do < 2:
                w.append(fr.get("mm3"))
            for kk in range(NDC):
                temit(lambda e, do=do, kk=kk, bk=bk: e.matmul(
                    bank(bk, T3), wov[:, kk, do * 128:(do + 1) * 128],
                    yg3[:, kk * T3:(kk + 1) * T3],
                    start=(kk == 0), stop=(kk == NDC - 1)),
                    waits=w if kk == 0 else ())
            tmm = b.tok("T")
            vemit(lambda e, do=do, bk=bk: e.scalar_tensor_tensor(
                xr3[:, do * T3:(do + 1) * T3], bank(bk, T3),
                bout_sb[:, do:do + 1], u3[:, do * T3:(do + 1) * T3],
                ALU.add, ALU.add),
                waits=[tmm, tok_u] + ([fr.get("xr")] if do == 0 else []))
            xev[do] = b.tok("V")
        tok_xr = b.tok("V")
        fr["mm3"] = tok_xr

        sq_t = None
        for c in range(NDM):
            sph = c % 2
            w = [tok_xr] + ([fr.get("rp")] if c == 0 else [])
            if sq_t is not None and c >= 2:
                w.append(sq_t)
            stok = semit(lambda e, c=c, sph=sph: e.activation(
                sq3a[:, sph * T3:(sph + 1) * T3], xr3[:, c * T3:(c + 1) * T3],
                ACT.Square), waits=w)
            temit(lambda e, c=c: e.matmul(bank(2)[0:1, 0:T3], ones_col[:, :],
                                          xr3[:, c * T3:(c + 1) * T3],
                                          start=(c == 0), stop=(c == NDM - 1)),
                  waits=[tok_xr] if c == 0 else ())
            temit(lambda e, c=c, sph=sph: e.matmul(
                bank(5)[0:1, 0:T3], ones_col[:, :], sq3a[:, sph * T3:(sph + 1) * T3],
                start=(c == 0), stop=(c == NDM - 1)), waits=[stok])
            sq_t = b.tok("T")
        tok_st = b.tok("T")
        rm = rp3[0:1, 0:T3]
        rv = rp3[0:1, T3:2 * T3]
        rq = rp3[0:1, 2 * T3:3 * T3]
        w = [tok_st] + ([fr.get("rp")] if "rp" in fr else [])
        vemit(lambda e: e.tensor_scalar_mul(rm, bank(2)[0:1, 0:T3], 1.0 / D), waits=w)
        vemit(lambda e: e.tensor_scalar_mul(rv, bank(5)[0:1, 0:T3], 1.0 / D))
        vemit(lambda e: e.tensor_mul(rq, rm, rm))
        tok_v3 = vemit(lambda e: e.tensor_sub(rv, rv, rq))
        semit(lambda e: e.activation(rv, rv, ACT.Ln, bias=eps_sb[0:1, :]),
              waits=[tok_v3])
        tok_rstd3 = semit(lambda e: e.activation(rv, rv, ACT.Exp, scale=-0.5))
        tok_mr3 = vemit(lambda e: e.tensor_mul(rm, rm, rv), waits=[tok_rstd3])
        temit(lambda e: e.matmul(bank(3, T3), ones_row[0:1, :], rv,
                                 start=True, stop=True), waits=[tok_mr3])
        tok_rep3 = temit(lambda e: e.matmul(bank(4, T3), ones_row[0:1, :], rm,
                                            start=True, stop=True))
        fr["rp"] = tok_rep3
        for c in range(NDM):
            vemit(lambda e, c=c: e.tensor_mul(xh3[:, c * T3:(c + 1) * T3],
                                              xr3[:, c * T3:(c + 1) * T3],
                                              bank(3, T3)),
                  waits=[tok_rep3] + ([fr.get("xh3w")] if c == 0 else []))
            vemit(lambda e, c=c: e.tensor_sub(xh3[:, c * T3:(c + 1) * T3],
                                              xh3[:, c * T3:(c + 1) * T3],
                                              bank(4, T3)))
        tok_xh3 = b.tok("V")

        tok_spxr = b.dma("gpsimd", cpt(p_xr2[:, :, tof:tof + T3]),
                         r2(xr3, NDM, T3), waits=[tok_xh3, fr.get("spxr")])
        tok_spxh = b.dma("gpsimd", cpt(p_xh2[:, :, tof:tof + T3]),
                         r2(xh3, NDM, T3), waits=[fr.get("spxh")])
        fr["spxr"] = tok_spxr
        fr["spxh"] = tok_spxh
        fr["xr"] = tok_spxr
        fr["yg"] = tok_spxr
        fr["xh3"] = tok_spxh
        fr["xh3w"] = tok_spxh
        fr["big1"] = tok_spxh
        fr["big2"] = tok_spxr

    ph3a_done = join_all()

    b.dma("sync", r2(wg_sb, 8, NE), wg_d.rearrange("(k p) f -> p k f", p=128),
          waits=ph3a_done)
    b.dma("sync", r2(acat_sb, 8, NE * NR),
          acat_d.rearrange("(k p) f -> p k f", p=128))
    tok_wcst = b.tok("LD")

    xh4_tok = [None, None]
    ff = {}

    def ld3b(i):
        slot = i % 2
        xh4_tok[slot] = b.dma("sync", r2(xh4[slot], NDM, T3),
                              cpt(p_xh2[:, :, i * T3:(i + 1) * T3]),
                              waits=[tok_wcst, ff.get(("xh4", slot))] +
                              (ph3a_done if i == 0 else []))

    ld3b(0)
    wic_tok = [None, None]
    woc_tok = [None, None]
    bcc_tok = [None, None]
    wgv = r2(wg_sb, 8, NE)
    acv = r2(acat_sb, 8, NE * NR)
    xr4_tok = None

    for i in range(N3):
        slot = i % 2
        if i + 1 < N3:
            ld3b(i + 1)
        xhv = xh4[slot]
        tof = i * T3

        for kk in range(NDM):
            temit(lambda e, kk=kk, xhv=xhv: e.matmul(
                bank(5, T3)[0:NE, :], wgv[:, kk, :], xhv[:, kk * T3:(kk + 1) * T3],
                start=(kk == 0), stop=(kk == NDM - 1)),
                waits=([xh4_tok[slot], tok_wcst, ff.get("b5")] if kk == 0 else ()))
        tok_lg = b.tok("T")
        tok_lev4 = semit(lambda e: e.activation(lgt4[0:NE, :], bank(5, T3)[0:NE, :],
                                                ACT.Copy),
                         waits=[tok_lg, ff.get("lgt4")])
        tok_lgr = b.dma("sync", lgr0.rearrange("o (n t) -> o n t", n=NE),
                        lgt4[0:NE, :], waits=[tok_lev4, ff.get("lgr")])
        ff["lgt4"] = tok_lgr
        tok_bgadd = vemit(lambda e: e.tensor_tensor(
            lgr0.rearrange("o (n t) -> o n t", n=NE),
            lgr0.rearrange("o (n t) -> o n t", n=NE),
            bg_sb[0:1, :].rearrange("o (n a) -> o n a", n=NE)
            .broadcast_to((1, NE, T3)), ALU.add),
            waits=[tok_lgr, ff.get("el")])
        tok_el = semit(lambda e: e.activation(expl0[:, :], lgr0[:, :], ACT.Exp),
                       waits=[tok_bgadd])

        def E(e_i):
            return expl0[:, e_i * T3:(e_i + 1) * T3]

        ta = ts0[:, 0:T3]
        tb = ts0[:, T3:2 * T3]
        tsum = ts0[:, 2 * T3:3 * T3]
        trinv = ts0[:, 3 * T3:4 * T3]
        w = [tok_el, ff.get("tsc")]
        vemit(lambda e: e.tensor_add(tsum, E(0), E(1)), waits=w)
        vemit(lambda e: e.tensor_add(ta, E(2), E(3)))
        vemit(lambda e: e.tensor_add(tsum, tsum, ta))
        vemit(lambda e: e.reciprocal(trinv, tsum))
        W4 = [w40[:, e_i * T3:(e_i + 1) * T3] for e_i in range(NE)]
        w = [ff.get("w40")]
        vemit(lambda e: e.tensor_tensor(W4[0], E(0), E(1), ALU.max), waits=w)
        vemit(lambda e: e.tensor_tensor(W4[1], E(0), E(1), ALU.min))
        vemit(lambda e: e.tensor_tensor(W4[2], E(2), E(3), ALU.max))
        vemit(lambda e: e.tensor_tensor(W4[3], E(2), E(3), ALU.min))
        vemit(lambda e: e.tensor_tensor(ta, W4[0], W4[2], ALU.min))
        vemit(lambda e: e.tensor_tensor(tb, W4[1], W4[3], ALU.max))
        tok_m2 = vemit(lambda e: e.tensor_tensor(ta, ta, tb, ALU.max))
        vemit(lambda e: e.tensor_tensor(
            w40[:, :].rearrange("o (n t) -> o n t", n=NE),
            expl0[:, :].rearrange("o (n t) -> o n t", n=NE),
            ta.rearrange("o (a t) -> o a t", a=1).broadcast_to((1, NE, T3)),
            ALU.is_ge))
        vemit(lambda e: e.tensor_mul(w40[:, :], w40[:, :], expl0[:, :]))
        tok_w4 = vemit(lambda e: e.tensor_tensor(
            w40[:, :].rearrange("o (n t) -> o n t", n=NE),
            w40[:, :].rearrange("o (n t) -> o n t", n=NE),
            trinv.rearrange("o (a t) -> o a t", a=1).broadcast_to((1, NE, T3)),
            ALU.mult))
        ff["el"] = tok_w4
        ff["tsc"] = tok_w4
        ff["lgr"] = tok_w4

        wev = []
        for e_i in range(NE):
            bk = e_i % 2
            w = [tok_w4]
            if len(wev) >= 2:
                w.append(wev[-2])
            elif ff.get("b01") is not None:
                w.append(ff.get("b01"))
            temit(lambda e, e_i=e_i, bk=bk: e.matmul(
                bank(bk, T3), ones_row[0:1, :], w40[:, e_i * T3:(e_i + 1) * T3],
                start=True, stop=True), waits=w)
            tmm = b.tok("T")
            semit(lambda e, e_i=e_i, bk=bk: e.activation(
                wrep[:, e_i * T3:(e_i + 1) * T3], bank(bk, T3), ACT.Copy),
                waits=[tmm])
            wev.append(b.tok("S"))
        tok_wrep = b.tok("S")
        ff["w40"] = tok_wrep

        for kk in range(NDM):
            temit(lambda e, kk=kk, xhv=xhv: e.matmul(
                bank(5, T3)[0:NE * NR, :], acv[:, kk, :],
                xhv[:, kk * T3:(kk + 1) * T3],
                start=(kk == 0), stop=(kk == NDM - 1)),
                waits=([tok_el] if kk == 0 else ()))
        tok_llm = b.tok("T")
        ff["b5"] = tok_llm
        tok_ll = semit(lambda e: e.activation(llsb16[0:NE * NR, :],
                                              bank(5, T3)[0:NE * NR, :], ACT.Copy),
                       waits=[tok_llm, ff.get("llsb")])

        gdone = []
        for fc in range(NFC):
            ws = fc % 2
            wic_tok[ws] = b.dma("sync", r2(wic[ws], 8, 128),
                                wi_d[:, fc * 128:(fc + 1) * 128]
                                .rearrange("(k p) f -> p k f", p=128),
                                waits=[wic_tok[ws],
                                       gdone[-2] if len(gdone) >= 2 else None] +
                                ([tok_wcst] if fc < 2 else []))
            bcc_tok[ws] = b.dma("sync", bcatc[ws][0:16, :],
                                bcat_d[:, fc * NE * 128:(fc + 1) * NE * 128],
                                waits=[bcc_tok[ws],
                                       gdone[-2] if len(gdone) >= 2 else None])
            for kk in range(NDM):
                temit(lambda e, kk=kk, ws=ws, xhv=xhv: e.matmul(
                    bank(2, T3), wic[ws][:, kk * 128:(kk + 1) * 128],
                    xhv[:, kk * T3:(kk + 1) * T3],
                    start=(kk == 0), stop=(kk == NDM - 1)),
                    waits=([wic_tok[ws]] + ([gdone[-1]] if gdone else []))
                    if kk == 0 else ())
            tok_base = b.tok("T")
            for e_i in range(NE):
                temit(lambda e, e_i=e_i, ws=ws: e.matmul(
                    bank(3 + e_i, T3), bcatc[ws][0:16, e_i * 128:(e_i + 1) * 128],
                    llsb16[0:16, :],
                    start=True, stop=True),
                    waits=([bcc_tok[ws], tok_ll, tok_w4] +
                           ([gdone[-1]] if gdone else [])) if e_i == 0 else ())
            tok_lora = b.tok("T")
            semit(lambda e: e.activation(lor4[:, :], ps[:, 3 * 512:3 * 512 + NE * T3],
                                         ACT.Copy), waits=[tok_lora, (
                                             gdone[-1] if gdone else None)])
            tok_lev = b.tok("S")
            vemit(lambda e: e.scalar_tensor_tensor(
                r2(hp4, NE, T3),
                bank(2, T3).rearrange("p (a t) -> p a t", a=1)
                .broadcast_to((128, NE, T3)),
                1.0, r2(lor4, NE, T3), ALU.mult, ALU.add),
                waits=[tok_base, tok_lev] +
                ([ff.get("outd")] if fc == 0 else []))
            tok_hp = b.tok("V")
            tok_hr = semit(lambda e: e.activation(hr4[:, :], hp4[:, :], ACT.Relu),
                           waits=[tok_hp])
            vemit(lambda e: e.tensor_tensor(
                hw4[:, :].rearrange("p (t n) -> p n t", n=NE),
                r2(hr4, NE, T3), r2(wrep, NE, T3), ALU.mult),
                waits=[tok_hr, tok_wrep])
            vemit(lambda e, fc=fc: e.tensor_reduce(
                g4[:, fc * T3:(fc + 1) * T3],
                hw4[:, :].rearrange("p (t n) -> p t n", n=NE), AX.X, ALU.add))
            gdone.append(b.tok("V"))
        tok_g = b.tok("V")
        ff["llsb"] = tok_g

        if debug:
            b.dma("gpsimd", cpt(p_dg[:, :, tof:tof + T3]), r2(g4, NFC, T3),
                  waits=[tok_g])
            b.dma("gpsimd", p_dw[0:1, :].rearrange("o (n t) -> o n t", n=NE)
                  [:, :, tof:tof + T3],
                  w40.rearrange("o (n t) -> o n t", n=NE), waits=[tok_g])
            b.dma("gpsimd", p_dll[:, tof:tof + T3], llsb16[0:16, :], waits=[tok_g])
            b.dma("gpsimd", cpt(p_dbase[:, :, tof:tof + T3]), r2(hp4, NE, T3),
                  waits=[tok_g])
        xr4_tok = b.dma("sync", r2(xr4, NDM, T3), cpt(p_xr2[:, :, tof:tof + T3]),
                        waits=[ff.get("xr4")])

        wo_mm = {}
        for kk in range(NFC):
            ws = kk % 2
            woc_tok[ws] = b.dma("sync", woc[ws][:, :],
                                wo_d[kk * 128:(kk + 1) * 128, :],
                                waits=[woc_tok[ws], tok_g, wo_mm.get(kk - 2)])
            for do in range(NDM):
                w = ()
                if do == 0:
                    w = [woc_tok[ws], tok_g]
                    if kk == 0:
                        w.append(ff.get("out"))
                temit(lambda e, kk=kk, do=do, ws=ws: e.matmul(
                    bank(do, T3), woc[ws][:, do * 128:(do + 1) * 128],
                    g4[:, kk * T3:(kk + 1) * T3],
                    start=(kk == 0), stop=(kk == NFC - 1)), waits=w)
            wo_mm[kk] = b.tok("T")
        tok_wo = b.tok("T")
        for do in range(NDM):
            vemit(lambda e, do=do: e.tensor_add(
                out4[:, do * T3:(do + 1) * T3], bank(do, T3),
                xr4[:, do * T3:(do + 1) * T3]),
                waits=[tok_wo, xr4_tok, ff.get("outd")] if do == 0 else ())
        tok_out = b.tok("V")
        ff[("xh4", slot)] = tok_g
        ff["xr4"] = tok_out
        ff["b5"] = tok_out
        ff["b01"] = tok_out
        ff["out"] = b.dma("sync", out_v[:, :, tof:tof + T3], r2(out4, NDM, T3),
                          waits=[tok_out])
        ff["outd"] = ff["out"]

    if debug:
        allj = join_all()
        dbg_car = nc.dram_tensor("d_car", [128, NDC * NMY], F32,
                                 kind="ExternalOutput").ap()
        b.dma("sync", dbg_car[:, :], car_sb[:, :], waits=allj)
        dbg = {
            "d_delta": (sh_delta, [2, NDC, 128, TH]),
            "d_deltax": (sh_deltax, [2, NDC, 128, TH]),
            "d_bc": (sh_bc, [2, 32, TH]),
            "d_yp": (p_yp, [NDC, 128, TH]),
            "d_shyp": (sh_yp, [2, NDC, 128, TH]),
            "d_s": (p_s, [NDC, 128, TH]),
            "d_g1": (p_g1, [NDC, 128, TH]),
            "d_rs": (p_rs, [1, 2 * TH]),
            "d_xr2": (p_xr2, [NDM, 128, TH]),
            "d_dy": (p_dy, [NDC, 128, TH]),
            "d_dg": (p_dg, [NFC, 128, TH]),
            "d_dw": (p_dw, [1, NE * TH]),
            "d_dll": (p_dll, [16, TH]),
            "d_dbase": (p_dbase, [NE, 128, TH]),
            "d_dyg": (p_dyg, [NDC, 128, TH]),
            "d_xh2": (p_xh2, [NDM, 128, TH]),
        }
        for nm, (src_ap, shp) in dbg.items():
            od = nc.dram_tensor(nm, shp, F32, kind="ExternalOutput").ap()
            b.dma("sync", od[tuple(slice(None) for _ in shp)], src_ap[
                tuple(slice(None) for _ in shp)], waits=allj)

    final_ld = b.tok("LD")
    final_gd = b.tok("GD")
    b.emit("sync", lambda e: None, waits=[final_ld], inc=None)
    b.emit("gpsimd", lambda e: None, waits=[final_gd], inc=None)

    with (
        nc.semaphore("sLD") as s_ld,
        nc.semaphore("sGD") as s_gd,
        nc.semaphore("sT") as s_t,
        nc.semaphore("sS") as s_s,
        nc.semaphore("sV") as s_v,
        nc.semaphore("sBAR") as s_bar,
        nc.Block() as block,
    ):
        b.sems.update({"LD": s_ld, "GD": s_gd, "T": s_t, "S": s_s, "V": s_v,
                       "BAR": s_bar})
        b.regs.update({"sync": nc.sync.alloc_register("par_sync"),
                       "gpsimd": nc.gpsimd.alloc_register("par_gp")})

        def mk(eng):
            def body(e):
                for fn, waits, inc, amt in b.q[eng]:
                    for (sn, val) in waits:
                        e.wait_ge(b.sems[sn], val)
                    ins = fn(e)
                    if inc is not None and ins is not None:
                        ins.then_inc(b.sems[inc], amt)
            return body

        block.sync(mk("sync"))
        block.scalar(mk("scalar"))
        block.vector(mk("vector"))
        block.tensor(mk("tensor"))
        block.gpsimd(mk("gpsimd"))

    return nc


_CACHE = {}


def _get_nc(debug=False):
    key = "ncd" if debug else "nc"
    if key not in _CACHE:
        _CACHE[key] = build_program(debug)
    return _CACHE[key]


def _prep_inputs(inputs):
    def c(x):
        return np.ascontiguousarray(np.asarray(x), dtype=np.float32)

    u = c(inputs["L0_attn_u"])
    W_in = c(inputs["L0_attn_W_in"])
    b_in = c(inputs["L0_attn_b_in"])
    W_x = c(inputs["L0_attn_W_x"])
    b_x = c(inputs["L0_attn_b_x"])
    W_dt = c(inputs["L0_attn_W_dt"])
    b_dt = c(inputs["L0_attn_b_dt"])
    A_log = c(inputs["L0_attn_A_log"])
    Dv = c(inputs["L0_attn_D"])
    W_out = c(inputs["L0_attn_W_out"])
    b_out = c(inputs["L0_attn_b_out"])
    Wg = c(inputs["L0_ffn_Wg"])
    bg = c(inputs["L0_ffn_bg"])
    Wi = c(inputs["L0_ffn_Wi"])
    Wo = c(inputs["L0_ffn_Wo"])
    Af = c(inputs["L0_ffn_A"])
    Bf = c(inputs["L0_ffn_B"])

    bcat = np.zeros((16, NFC, NE, 128), np.float32)
    for e_i in range(NE):
        bcat[e_i * NR:(e_i + 1) * NR, :, e_i, :] = (
            Bf[e_i].reshape(NFC, 128, NR).transpose(2, 0, 1))
    bcat = bcat.reshape(16, NFC * NE * 128)

    shared = {
        "winx": np.ascontiguousarray(W_in[:DI].T),
        "winz": np.ascontiguousarray(W_in[DI:].T),
        "binx": np.ascontiguousarray(b_in[:DI].reshape(NDC, 128).T),
        "binz": np.ascontiguousarray(b_in[DI:].reshape(NDC, 128).T),
        "wx": np.ascontiguousarray(W_x.T),
        "bx": np.ascontiguousarray(b_x.reshape(96, 1)),
        "wdt": np.ascontiguousarray(W_dt.T),
        "bdt": np.ascontiguousarray(b_dt.reshape(NDC, 128).T),
        "dvec": np.ascontiguousarray(Dv.reshape(NDC, 128).T),
        "wout": np.ascontiguousarray(W_out.T),
        "bout": np.ascontiguousarray(b_out.reshape(NDM, 128).T),
        "wg": np.ascontiguousarray(Wg.T),
        "bg": np.ascontiguousarray(bg.reshape(1, NE)),
        "wi": np.ascontiguousarray(Wi.T),
        "acat": np.ascontiguousarray(Af.reshape(NE * NR, D).T),
        "bcat": np.ascontiguousarray(bcat),
        "wo": np.ascontiguousarray(Wo.T),
    }

    in_maps = []
    for c_id in range(8):
        bidx, q = c_id // 2, c_id % 2
        m = dict(shared)
        m["u"] = np.ascontiguousarray(u[bidx, q * TH:(q + 1) * TH, :].T)
        m["par"] = np.full((1, 1), q, np.int32)
        ns = list(range(q * NMY, (q + 1) * NMY))
        amy = A_log[:, ns].reshape(NDC, 128, NMY).transpose(1, 0, 2)
        m["amy"] = np.ascontiguousarray(amy.reshape(128, NDC * NMY))
        sel = np.zeros((32, 2 * NMY * 128), np.float32)
        for jn, gn in enumerate(ns):
            sel[gn, jn * 128:(jn + 1) * 128] = 1.0
            sel[16 + gn, (NMY + jn) * 128:(NMY + jn + 1) * 128] = 1.0
        m["sel"] = sel
        in_maps.append(m)
    return in_maps


def kernel(**inputs):
    nc = _get_nc()
    in_maps = _prep_inputs(inputs)
    res = run_bass_kernel_spmd(nc, in_maps, core_ids=list(range(8)))
    out = np.empty((4, TF, D), np.float32)
    for c_id in range(8):
        bidx, q = c_id // 2, c_id % 2
        out[bidx, q * TH:(q + 1) * TH, :] = res.results[c_id]["out"].T
    return out


# version 05 (final)
# speedup vs baseline: 1.1684x; 1.1684x over previous
import numpy as np
import concourse.bass as bass
import concourse.mybir as mybir
from concourse.bass_utils import run_bass_kernel_spmd

F32 = mybir.dt.float32
I32 = mybir.dt.int32
ALU = mybir.AluOpType
ACT = mybir.ActivationFunctionType
AX = mybir.AxisListType

D = 1024
DI = 2048
DTR = 64
NS = 16
NMY = 8
TH = 4096
TF = 8192
FF = 4096
NE = 4
NR = 4

T1X = 256
N1X = TH // T1X
T1Z = 512
N1Z = TH // T1Z
T2 = 512
N2 = TF // T2
T3 = 512
N3 = TH // T3

NDC = DI // 128
NDM = D // 128
NFC = FF // 128

ENGS = ("sync", "scalar", "vector", "tensor", "gpsimd")
PAIRS = [[0, 1], [2, 3], [4, 5], [6, 7]]


class Build:
    def __init__(self, nc, semnames):
        self.nc = nc
        self.q = {e: [] for e in ENGS}
        self.cnt = {s: 0 for s in semnames}
        self.sems = {}
        self.regs = {}

    def emit(self, eng, fn, waits=(), inc=None, amt=1):
        ws = tuple(w for w in waits if w is not None)
        self.q[eng].append((fn, ws, inc, amt))
        if inc is not None:
            self.cnt[inc] += amt
            return (inc, self.cnt[inc])
        return None

    def dma(self, eng, out, in_, waits=(), sem=None):
        sem = sem or ("GD" if eng == "gpsimd" else "LD")

        def fn(e, out=out, in_=in_):
            return e.dma_start(out=out, in_=in_)
        return self.emit(eng, fn, waits, sem, 16)

    def dma_if(self, eng, out0, in0, out1, in1, waits=(), sem=None):
        sem = sem or ("GD" if eng == "gpsimd" else "LD")
        semn, regs, sems = sem, self.regs, self.sems

        def fn(e, out0=out0, in0=in0, out1=out1, in1=in1):
            with e.If_eq(regs[eng], 0):
                e.dma_start(out=out0, in_=in0).then_inc(sems[semn], 16)
            with e.Else():
                e.dma_start(out=out1, in_=in1).then_inc(sems[semn], 16)
            return None
        return self.emit(eng, fn, waits, semn, 16)

    def tok(self, sem):
        return (sem, self.cnt[sem])


def build_program(debug=False):
    nc = bass.Bass("TRN2", target_bir_lowering=False)

    def din(name, shape, dt=F32):
        return nc.dram_tensor(name, list(shape), dt, kind="ExternalInput").ap()

    u_d = din("u", [D, TH])
    par_d = din("par", [1, 1], I32)
    winx_d = din("winx", [D, DI])
    winz_d = din("winz", [D, DI])
    binx_d = din("binx", [128, NDC])
    binz_d = din("binz", [128, NDC])
    wx_d = din("wx", [DI, 96])
    bx_d = din("bx", [96, 1])
    wdt_d = din("wdt", [DTR, DI])
    bdt_d = din("bdt", [128, NDC])
    amy_d = din("amy", [128, NDC * NMY])
    dvec_d = din("dvec", [128, NDC])
    sel_d = din("sel", [32, 2 * NMY * 128])
    wout_d = din("wout", [DI, D])
    bout_d = din("bout", [128, NDM])
    wg_d = din("wg", [D, NE])
    bg_d = din("bg", [1, NE])
    wi_d = din("wi", [D, FF])
    acat_d = din("acat", [D, NE * NR])
    bcat_d = din("bcat", [16, NFC * NE * 128])
    wo_d = din("wo", [FF, D])

    out_d = nc.dram_tensor("out", [D, TH], F32, kind="ExternalOutput").ap()
    out_v = out_d.rearrange("(c p) t -> p c t", p=128)

    def shr(name, shape):
        return nc.dram_tensor(name, list(shape), F32, kind="Internal",
                              addr_space="Shared").ap()

    def pri(name, shape):
        return nc.dram_tensor(name, list(shape), F32, kind="Internal").ap()

    sh_delta = shr("sh_delta", [2, NDC, 128, TH])
    sh_deltax = shr("sh_deltax", [2, NDC, 128, TH])
    sh_bc = shr("sh_bc", [2, 32, TH])
    sh_yp = shr("sh_yp", [2, NDC, 128, TH])
    p_yp = pri("p_yp", [NDC, 128, TH])
    p_s = pri("p_s", [NDC, 128, TH])
    p_g1 = pri("p_g1", [NDC, 128, TH])
    p_rs = pri("p_rs", [1, 2 * TH])
    p_xr2 = pri("p_xr2", [NDM, 128, TH])
    p_xh2 = pri("p_xh2", [NDM, 128, TH])
    p_dy = pri("p_dy", [NDC, 128, TH])
    p_dg = pri("p_dg", [NFC, 128, TH])
    p_dw = pri("p_dw", [1, NE * TH])
    p_dll = pri("p_dll", [16, TH])
    p_dbase = pri("p_dbase", [NE, 128, TH])
    p_dyg = pri("p_dyg", [NDC, 128, TH])
    bar_i = pri("bar_i", [1, 1])
    bar_o = pri("bar_o", [1, 1])

    ARENA_COLS = 52000
    arena = nc.alloc_sbuf_tensor("arena", [128, ARENA_COLS], F32).ap()
    par_sb = nc.alloc_sbuf_tensor("par_sb", [1, 1], I32).ap()

    class Alloc:
        def __init__(self, off=0):
            self.off = off

        def a(self, cols):
            assert self.off + cols <= ARENA_COLS, (self.off, cols)
            v = arena[:, self.off:self.off + cols]
            self.off += cols
            return v

    AL = Alloc()
    ones_col = AL.a(1)
    ones_row = AL.a(128)
    binx_sb = AL.a(NDC)
    binz_sb = AL.a(NDC)
    bx_sb = AL.a(1)
    bdt_sb = AL.a(NDC)
    amy_sb = AL.a(NDC * NMY)
    dvec_sb = AL.a(NDC)
    bout_sb = AL.a(NDM)
    bg_sb = AL.a(NE)
    eps_sb = AL.a(1)
    car_sb = AL.a(NDC * NMY)
    PERM = AL.off

    A = Alloc(PERM)
    winx_sb = A.a(8 * DI)
    wx_sb = A.a(NDC * 96)
    wdt_sb = A.a(DI)
    u1 = [A.a(8 * T1X) for _ in range(2)]
    usq1 = A.a(8 * T1X)
    xh1 = A.a(8 * T1X)
    xsb = A.a(NDC * T1X)
    delt1 = A.a(NDC * T1X)
    dtmp1 = A.a(T1X)
    lntmp = A.a(3 * T1X)
    g11 = A.a(NDC * T1X)
    smreg = A.a(2 * TH)
    assert A.off <= ARENA_COLS, A.off
    rsacc = smreg[0:1, :]
    bcacc = smreg[32:64, 0:TH]

    A = Alloc(PERM)
    winz_sb = A.a(8 * DI)
    u2 = [A.a(8 * T1Z) for _ in range(2)]
    xh1z = A.a(8 * T1Z)
    rs2 = A.a(2 * T1Z)
    ssb = A.a(NDC * T1Z)
    assert A.off <= ARENA_COLS

    A = Alloc(PERM)
    sel_sb = A.a(2 * NMY * 128)
    bct = A.a(T2)
    dlt2 = [A.a(T2) for _ in range(4)]
    dxt2 = [A.a(T2) for _ in range(4)]
    brep = A.a(NMY * T2)
    crep = A.a(NMY * T2)
    da2 = [A.a(NMY * T2) for _ in range(2)]
    b2 = A.a(NMY * T2)
    h2 = A.a(NMY * T2)
    hc2 = A.a(NMY * T2)
    y2 = A.a(NDC * T2)
    assert A.off <= ARENA_COLS

    A = Alloc(PERM)
    wout_sb = A.a(NDC * D)
    big1 = A.a(NDC * T3)
    big2 = A.a(NDC * T3)
    yg3 = A.a(NDC * T3)
    sq3a = A.a(2 * T3)
    rp3 = A.a(4 * T3)
    assert A.off <= ARENA_COLS
    xh3 = big1[:, 0:NDM * T3]
    u3 = big1[:, NDM * T3:2 * NDM * T3]
    xr3 = big2[:, 0:NDM * T3]

    A = Alloc(PERM)
    wg_sb = A.a(8 * NE)
    acat_sb = A.a(8 * NE * NR)
    bcatc = [A.a(NE * 128) for _ in range(2)]
    xh4 = [A.a(NDM * T3) for _ in range(2)]
    xr4 = A.a(NDM * T3)
    wic = [A.a(8 * 128) for _ in range(2)]
    woc = [A.a(D) for _ in range(2)]
    llsb16 = A.a(T3)
    lgt4 = A.a(T3)
    row0 = A.a(3 * NE * T3)
    wrep = A.a(NE * T3)
    lor4 = [A.a(NE * T3) for _ in range(2)]
    hpr = A.a(2 * NE * T3)
    hp4 = hpr[:, 0:NE * T3]
    hr4 = hpr[:, NE * T3:2 * NE * T3]
    out4 = hpr
    hw4 = hp4
    g4 = A.a(NFC * T3)
    assert A.off <= ARENA_COLS, A.off
    expl0 = row0[0:1, 0:NE * T3]
    w40 = row0[0:1, NE * T3:2 * NE * T3]
    ts0 = row0[0:1, 2 * NE * T3:3 * NE * T3]
    lgr0 = ts0

    ps = nc.alloc_psum_tensor("ps", [128, 4096], F32).ap()

    def bank(i, cols=512):
        return ps[:, i * 512:i * 512 + cols]

    SEMS = ("LD", "GD", "T", "S", "V", "BAR")
    b = Build(nc, SEMS)

    def r2(ap, c, t):
        return ap.rearrange("p (c t) -> p c t", c=c)

    def cpt(dram3):
        return dram3.rearrange("c p t -> p c t")

    def vemit(fn, waits=()):
        return b.emit("vector", fn, waits, "V")

    def semit(fn, waits=()):
        return b.emit("scalar", fn, waits, "S")

    def temit(fn, waits=()):
        return b.emit("tensor", fn, waits, "T")

    tok_par = b.dma("sync", par_sb[:, :], par_d[:, :])
    b.dma("sync", binx_sb[:, :], binx_d[:, :])
    b.dma("sync", binz_sb[:, :], binz_d[:, :])
    b.dma("sync", bx_sb[0:96, :], bx_d[:, :])
    b.dma("sync", bdt_sb[:, :], bdt_d[:, :])
    b.dma("sync", amy_sb[:, :], amy_d[:, :])
    b.dma("sync", dvec_sb[:, :], dvec_d[:, :])
    b.dma("sync", bout_sb[:, :], bout_d[:, :])
    b.dma("sync", bg_sb[0:1, :], bg_d[:, :])
    tok_consts = b.tok("LD")

    vemit(lambda e: e.memset(ones_col[:, :], 1.0), waits=[tok_consts])
    vemit(lambda e: e.memset(ones_row[0:1, :], 1.0))
    vemit(lambda e: e.memset(car_sb[:, :], 0.0))
    vemit(lambda e: e.memset(eps_sb[:, :], 1e-5))
    tok_ones = b.tok("V")
    tok_aexp = semit(lambda e: e.activation(amy_sb[:, :], amy_sb[:, :], ACT.Exp),
                     waits=[tok_consts])
    tok_amy = vemit(lambda e: e.tensor_scalar_mul(amy_sb[:, :], amy_sb[:, :], -1.0),
                    waits=[tok_aexp])

    for eng in ("sync", "gpsimd"):
        def fn(e, eng=eng):
            e.reg_load(b.regs[eng], par_sb[0:1, 0:1])
            return None
        b.emit(eng, fn, waits=[tok_par], inc=None)

    u_v = u_d.rearrange("(c p) t -> p c t", p=128)

    def join_all():
        return [b.tok("T"), b.tok("S"), b.tok("V"), b.tok("GD")]

    b.dma("sync", r2(winx_sb, 8, DI), winx_d.rearrange("(k p) f -> p k f", p=128))
    b.dma("sync", r2(wx_sb, NDC, 96), wx_d.rearrange("(k p) f -> p k f", p=128))
    b.dma("sync", wdt_sb[0:DTR, :].rearrange("p (c f) -> p c f", c=NDC),
          wdt_d.rearrange("k (c f) -> k c f", c=NDC))
    tok_w1x = b.tok("LD")

    u1_tok = [None, None]
    u1_free = [None, None]

    def ph1x_load(k):
        slot = k % 2
        u1_tok[slot] = b.dma("sync", r2(u1[slot], 8, T1X),
                             u_v[:, :, k * T1X:(k + 1) * T1X],
                             waits=[u1_free[slot]])

    ph1x_load(0)
    jtok = []
    spill_d = spill_dx = spill_g1 = None

    for k in range(N1X):
        slot = k % 2
        if k + 1 < N1X:
            ph1x_load(k + 1)
        ut = r2(u1[slot], 8, T1X)
        base_w = [u1_tok[slot], tok_w1x, tok_ones, tok_amy] + jtok

        tok_sq = semit(lambda e, slot=slot: e.activation(usq1[:, :], u1[slot][:, :],
                                                         ACT.Square), waits=base_w)
        for c in range(8):
            temit(lambda e, c=c, ut=ut: e.matmul(bank(2)[0:1, 0:T1X], ones_col[:, :],
                                                 ut[:, c, :], start=(c == 0),
                                                 stop=(c == 7)),
                  waits=base_w if c == 0 else ())
            temit(lambda e, c=c: e.matmul(bank(5)[0:1, 0:T1X], ones_col[:, :],
                                          usq1[:, c * T1X:(c + 1) * T1X],
                                          start=(c == 0), stop=(c == 7)),
                  waits=[tok_sq] if c == 0 else ())
        tok_stats = b.tok("T")
        rsl_r = rsacc[0:1, k * T1X:(k + 1) * T1X]
        rsl_m = rsacc[0:1, TH + k * T1X:TH + (k + 1) * T1X]
        lm = lntmp[0:1, 0:T1X]
        lv = lntmp[0:1, T1X:2 * T1X]
        lq = lntmp[0:1, 2 * T1X:3 * T1X]
        vemit(lambda e, lm=lm: e.tensor_scalar_mul(lm, bank(2)[0:1, 0:T1X],
                                                   1.0 / D), waits=[tok_stats])
        vemit(lambda e, lv=lv: e.tensor_scalar_mul(lv, bank(5)[0:1, 0:T1X],
                                                   1.0 / D))
        vemit(lambda e, lm=lm, lq=lq: e.tensor_mul(lq, lm, lm))
        tok_var = vemit(lambda e, lv=lv, lq=lq: e.tensor_sub(lv, lv, lq))
        semit(lambda e, lv=lv: e.activation(lv, lv, ACT.Ln,
                                            bias=eps_sb[0:1, :]), waits=[tok_var])
        tok_rstd = semit(lambda e, rsl_r=rsl_r, lv=lv: e.activation(
            rsl_r, lv, ACT.Exp, scale=-0.5))
        tok_mr = vemit(lambda e, rsl_r=rsl_r, rsl_m=rsl_m, lm=lm: e.tensor_mul(
            rsl_m, lm, rsl_r), waits=[tok_rstd])
        temit(lambda e, rsl_r=rsl_r: e.matmul(bank(3, T1X), ones_row[0:1, :], rsl_r,
                                              start=True, stop=True), waits=[tok_mr])
        tok_rep = temit(lambda e, rsl_m=rsl_m: e.matmul(bank(4, T1X), ones_row[0:1, :],
                                                        rsl_m, start=True, stop=True))
        for c in range(8):
            vemit(lambda e, c=c, ut=ut: e.tensor_mul(xh1[:, c * T1X:(c + 1) * T1X],
                                                     ut[:, c, :], bank(3, T1X)),
                  waits=[tok_rep] if c == 0 else ())
            vemit(lambda e, c=c: e.tensor_sub(xh1[:, c * T1X:(c + 1) * T1X],
                                              xh1[:, c * T1X:(c + 1) * T1X],
                                              bank(4, T1X)))
        tok_xh = b.tok("V")
        u1_free[slot] = tok_xh

        xw = r2(winx_sb, 8, DI)
        ev = {}
        for oc in range(NDC):
            bk = oc % 2
            w = [tok_xh, ev.get(oc - 2)]
            for kk in range(8):
                temit(lambda e, oc=oc, kk=kk, bk=bk: e.matmul(
                    bank(bk, T1X), xw[:, kk, oc * 128:(oc + 1) * 128],
                    xh1[:, kk * T1X:(kk + 1) * T1X],
                    start=(kk == 0), stop=(kk == 7)),
                    waits=w if kk == 0 else ())
            tmm = b.tok("T")
            semit(lambda e, oc=oc, bk=bk: e.activation(
                xsb[:, oc * T1X:(oc + 1) * T1X], bank(bk, T1X), ACT.Identity,
                bias=binx_sb[:, oc:oc + 1]), waits=[tmm])
            semit(lambda e, oc=oc: e.activation(
                g11[:, oc * T1X:(oc + 1) * T1X], xsb[:, oc * T1X:(oc + 1) * T1X],
                ACT.Copy, scale=dvec_sb[:, oc:oc + 1]))
            ev[oc] = b.tok("S")
        tok_xall = b.tok("S")

        wxv = r2(wx_sb, NDC, 96)
        for kk in range(NDC):
            temit(lambda e, kk=kk: e.matmul(
                bank(6, T1X)[0:96, :], wxv[:, kk, :],
                xsb[:, kk * T1X:(kk + 1) * T1X],
                start=(kk == 0), stop=(kk == NDC - 1)),
                waits=[tok_xall] if kk == 0 else ())
        tok_wx = b.tok("T")
        tok_dbc = semit(lambda e: e.activation(usq1[0:96, 0:T1X], bank(6, T1X)[0:96, :],
                                               ACT.Identity, bias=bx_sb[0:96, :]),
                        waits=[tok_wx])
        tok_bccp = vemit(lambda e, k=k: e.tensor_copy(
            bcacc[:, k * T1X:(k + 1) * T1X], usq1[64:96, 0:T1X]), waits=[tok_dbc])

        wdv = wdt_sb[0:DTR, :].rearrange("p (c f) -> p c f", c=NDC)
        dev = {}
        for oc in range(NDC):
            bk = oc % 2
            w = [tok_dbc, dev.get(oc - 2)]
            if oc < 2:
                w.append(ev[NDC - 2 + oc])
            temit(lambda e, oc=oc, bk=bk: e.matmul(
                bank(bk, T1X), wdv[:, oc, :], usq1[0:DTR, 0:T1X],
                start=True, stop=True), waits=w)
            tmm = b.tok("T")
            semit(lambda e, oc=oc, bk=bk: e.activation(
                dtmp1[:, :], bank(bk, T1X), ACT.Exp, bias=bdt_sb[:, oc:oc + 1]),
                waits=[tmm])
            semit(lambda e, oc=oc: e.activation(
                delt1[:, oc * T1X:(oc + 1) * T1X], dtmp1[:, :], ACT.Ln, bias=1.0))
            dev[oc] = b.tok("S")
        tok_delta = b.tok("S")
        tok_tend = b.tok("T")

        tok_dx = vemit(lambda e: e.tensor_mul(xsb[:, :], delt1[:, :], xsb[:, :]),
                       waits=[tok_delta, tok_xall])

        spill_d = b.dma_if(
            "gpsimd",
            cpt(sh_delta[0, :, :, k * T1X:(k + 1) * T1X]), r2(delt1, NDC, T1X),
            cpt(sh_delta[1, :, :, k * T1X:(k + 1) * T1X]), r2(delt1, NDC, T1X),
            waits=[tok_delta, spill_d])
        spill_dx = b.dma_if(
            "gpsimd",
            cpt(sh_deltax[0, :, :, k * T1X:(k + 1) * T1X]), r2(xsb, NDC, T1X),
            cpt(sh_deltax[1, :, :, k * T1X:(k + 1) * T1X]), r2(xsb, NDC, T1X),
            waits=[tok_dx, spill_dx])
        spill_g1 = b.dma("gpsimd", cpt(p_g1[:, :, k * T1X:(k + 1) * T1X]),
                         r2(g11, NDC, T1X), waits=[tok_xall, spill_g1])
        jtok = [spill_d, spill_dx, spill_g1, tok_dx, tok_bccp, tok_delta, tok_tend]

    b.dma_if("gpsimd", sh_bc[0, :, :], bcacc[:, :],
             sh_bc[1, :, :], bcacc[:, :], waits=[b.tok("V"), b.tok("S")])
    b.dma("gpsimd", p_rs[:, :], rsacc[:, :])
    ph1x_done = join_all()

    tok_wz = b.dma("sync", r2(winz_sb, 8, DI),
                   winz_d.rearrange("(k p) f -> p k f", p=128), waits=ph1x_done)
    u2_tok = [None, None]
    u2_free = [None, None]

    def ph1z_load(k):
        slot = k % 2
        u2_tok[slot] = b.dma("sync", r2(u2[slot], 8, T1Z),
                             u_v[:, :, k * T1Z:(k + 1) * T1Z],
                             waits=[u2_free[slot], tok_wz])

    ph1z_load(0)
    rs_tok = b.dma("sync",
                   rs2[0:1, :].rearrange("o (a t) -> o a t", a=2),
                   p_rs[0:1, :].rearrange("o (a t) -> o a t", a=2)[:, :, 0:T1Z],
                   waits=ph1x_done)
    rs_free = None
    spill_s = None
    mm_free = None

    for k in range(N1Z):
        slot = k % 2
        if k + 1 < N1Z:
            ph1z_load(k + 1)
        ut = r2(u2[slot], 8, T1Z)
        w = [u2_tok[slot], rs_tok, rs_free] + (ph1x_done if k == 0 else [])
        temit(lambda e: e.matmul(bank(3, T1Z), ones_row[0:1, :], rs2[0:1, 0:T1Z],
                                 start=True, stop=True), waits=w)
        tok_rep = temit(lambda e: e.matmul(bank(4, T1Z), ones_row[0:1, :],
                                           rs2[0:1, T1Z:2 * T1Z],
                                           start=True, stop=True))
        for c in range(8):
            vemit(lambda e, c=c, ut=ut: e.tensor_mul(xh1z[:, c * T1Z:(c + 1) * T1Z],
                                                     ut[:, c, :], bank(3, T1Z)),
                  waits=[tok_rep] if c == 0 else ())
            vemit(lambda e, c=c: e.tensor_sub(xh1z[:, c * T1Z:(c + 1) * T1Z],
                                              xh1z[:, c * T1Z:(c + 1) * T1Z],
                                              bank(4, T1Z)))
        tok_xh = b.tok("V")
        u2_free[slot] = tok_xh
        rs_free = tok_xh
        if k + 1 < N1Z:
            rs_tok = b.dma("sync",
                           rs2[0:1, :].rearrange("o (a t) -> o a t", a=2),
                           p_rs[0:1, :].rearrange("o (a t) -> o a t", a=2)
                           [:, :, (k + 1) * T1Z:(k + 2) * T1Z],
                           waits=[tok_rep])

        zw = r2(winz_sb, 8, DI)
        ev = {}
        for oc in range(NDC):
            bk = oc % 2
            w = [tok_xh, ev.get(oc - 2)]
            if oc < 2:
                w.append(mm_free)
            for kk in range(8):
                temit(lambda e, oc=oc, kk=kk, bk=bk: e.matmul(
                    bank(bk, T1Z), zw[:, kk, oc * 128:(oc + 1) * 128],
                    xh1z[:, kk * T1Z:(kk + 1) * T1Z],
                    start=(kk == 0), stop=(kk == 7)),
                    waits=w if kk == 0 else ())
            tmm = b.tok("T")
            semit(lambda e, oc=oc, bk=bk: e.activation(
                ssb[:, oc * T1Z:(oc + 1) * T1Z], bank(bk, T1Z), ACT.Silu,
                bias=binz_sb[:, oc:oc + 1]), waits=[tmm])
            ev[oc] = b.tok("S")
        mm_free = b.tok("S")
        spill_s = b.dma("gpsimd", cpt(p_s[:, :, k * T1Z:(k + 1) * T1Z]),
                        r2(ssb, NDC, T1Z), waits=[mm_free, spill_s])

    def barrier(waits):
        def fn(e):
            return e.collective_compute("AllReduce", ALU.add, replica_groups=PAIRS,
                                        ins=[bar_i[:, :]], outs=[bar_o[:, :]])
        return b.emit("gpsimd", fn, waits=waits, inc="BAR")

    tok_bar1 = barrier(join_all())

    pj = join_all() + [tok_bar1]
    tok_sel = b.dma("sync", sel_sb[0:32, :], sel_d[:, :], waits=pj)
    selv = sel_sb[0:32, :].rearrange("p (s f) -> p s f", s=2 * NMY)

    dl_tok = {}
    dl_free = {}
    next_load = [0]

    def ph2_issue_loads(upto):
        while next_load[0] <= min(upto, N2 * NDC - 1):
            gi = next_load[0]
            next_load[0] += 1
            jj, dcc = divmod(gi, NDC)
            slot = gi % 4
            half, tof = jj // 8, (jj % 8) * T2
            w = [tok_bar1]
            if slot in dl_free:
                w.extend(dl_free[slot])
            t1 = b.dma("sync", dlt2[slot][:, :], sh_delta[half, dcc, :, tof:tof + T2],
                       waits=w)
            t2 = b.dma("sync", dxt2[slot][:, :], sh_deltax[half, dcc, :, tof:tof + T2])
            dl_tok[gi] = (t1, t2)

    ph2_issue_loads(3)

    bct_tok = None
    bct_free = None
    da_slot = [0]
    da_free = [None, None]
    spill_y = None
    tile_v_end = None

    for j in range(N2):
        half, tof = j // 8, (j % 8) * T2
        bct_tok = b.dma("sync", bct[0:32, :], sh_bc[half, :, tof:tof + T2],
                        waits=[tok_bar1, bct_free])
        evs = []
        for n in range(NMY):
            for which, dst in ((0, brep), (1, crep)):
                bk = (2 * n + which) % 2
                w = [tok_sel, bct_tok]
                if len(evs) >= 2:
                    w.append(evs[-2])
                if len(evs) < 2 and tile_v_end is not None:
                    w.append(tile_v_end)
                soff = which * NMY
                temit(lambda e, n=n, soff=soff, bk=bk: e.matmul(
                    bank(bk, T2), selv[:, soff + n, :], bct[0:32, :],
                    start=True, stop=True), waits=w)
                tmm = b.tok("T")
                semit(lambda e, n=n, dst=dst, bk=bk: e.activation(
                    dst[:, n * T2:(n + 1) * T2], bank(bk, T2), ACT.Copy),
                    waits=[tmm])
                evs.append(b.tok("S"))
        tok_bcr = b.tok("S")
        bct_free = b.tok("T")

        for dc in range(NDC):
            gi = j * NDC + dc
            ph2_issue_loads(gi + 3)
            slot = gi % 4
            ltok = dl_tok[gi]

            das = da_slot[0]
            da_slot[0] = 1 - da_slot[0]
            dav = da2[das]
            for n in range(NMY):
                w = [ltok[0], tok_amy]
                if n == 0:
                    w.append(da_free[das])
                    if dc == 0:
                        w.append(tok_bcr)
                semit(lambda e, n=n, dc=dc, dav=dav, slot=slot: e.activation(
                    dav[:, n * T2:(n + 1) * T2], dlt2[slot][:, :], ACT.Exp,
                    scale=amy_sb[:, dc * NMY + n:dc * NMY + n + 1]),
                    waits=w if n == 0 else ())
            tok_da = b.tok("S")

            tok_b = vemit(lambda e, slot=slot: e.scalar_tensor_tensor(
                r2(b2, NMY, T2),
                dxt2[slot][:, :].rearrange("p (a t) -> p a t", a=1)
                .broadcast_to((128, NMY, T2)),
                1.0, r2(brep, NMY, T2), ALU.mult, ALU.mult),
                waits=[ltok[1], tok_bcr, tok_da])
            dl_free[slot] = (tok_da, tok_b)

            for n in range(NMY):
                vemit(lambda e, n=n, dav=dav, dc=dc: e.tensor_tensor_scan(
                    h2[:, n * T2:(n + 1) * T2], dav[:, n * T2:(n + 1) * T2],
                    b2[:, n * T2:(n + 1) * T2],
                    car_sb[:, dc * NMY + n:dc * NMY + n + 1],
                    ALU.mult, ALU.add))
            for n in range(NMY):
                vemit(lambda e, n=n, dc=dc: e.tensor_copy(
                    car_sb[:, dc * NMY + n:dc * NMY + n + 1],
                    h2[:, n * T2 + T2 - 1:n * T2 + T2]))
            da_free[das] = b.tok("V")
            vemit(lambda e: e.tensor_tensor(
                hc2[:, :].rearrange("p (t n) -> p n t", n=NMY),
                r2(h2, NMY, T2), r2(crep, NMY, T2), ALU.mult))
            w = [spill_y] if (dc == 0 and spill_y is not None) else ()
            vemit(lambda e, dc=dc: e.tensor_reduce(
                y2[:, dc * T2:(dc + 1) * T2],
                hc2[:, :].rearrange("p (t n) -> p t n", n=NMY), AX.X, ALU.add),
                waits=w)
        tile_v_end = b.tok("V")

        def fny(e, j=j, half=half, tof=tof):
            with e.If_eq(b.regs["gpsimd"], half):
                e.dma_start(out=cpt(p_yp[:, :, tof:tof + T2]),
                            in_=r2(y2, NDC, T2)).then_inc(b.sems["GD"], 16)
            with e.Else():
                e.dma_start(out=cpt(sh_yp[half, :, :, tof:tof + T2]),
                            in_=r2(y2, NDC, T2)).then_inc(b.sems["GD"], 16)
            return None
        spill_y = b.emit("gpsimd", fny, waits=[tile_v_end, spill_y], inc="GD", amt=16)

    tok_bar2 = barrier(join_all())

    pj = join_all() + [tok_bar2]
    tok_wo3 = b.dma("sync", r2(wout_sb, NDC, D),
                    wout_d.rearrange("(k p) f -> p k f", p=128), waits=pj)
    fr = {}
    for i in range(N3):
        tof = i * T3
        tok_yp1 = b.dma("sync", r2(big1, NDC, T3), cpt(p_yp[:, :, tof:tof + T3]),
                        waits=[tok_bar2, fr.get("big1")])

        def fnyp(e, tof=tof):
            with e.If_eq(b.regs["sync"], 0):
                e.dma_start(out=r2(big2, NDC, T3),
                            in_=cpt(sh_yp[0, :, :, tof:tof + T3])).then_inc(
                                b.sems["LD"], 16)
            with e.Else():
                e.dma_start(out=r2(big2, NDC, T3),
                            in_=cpt(sh_yp[1, :, :, tof:tof + T3])).then_inc(
                                b.sems["LD"], 16)
            return None
        tok_yp2 = b.emit("sync", fnyp, waits=[tok_bar2, fr.get("big2")],
                         inc="LD", amt=16)
        tok_y = vemit(lambda e: e.tensor_add(yg3[:, :], big1[:, :], big2[:, :]),
                      waits=[tok_yp1, tok_yp2, fr.get("yg")])
        tok_g1 = b.dma("sync", r2(big2, NDC, T3), cpt(p_g1[:, :, tof:tof + T3]),
                       waits=[tok_y])
        tok_yg1 = vemit(lambda e: e.tensor_add(yg3[:, :], yg3[:, :], big2[:, :]),
                        waits=[tok_g1])
        tok_s = b.dma("sync", r2(big2, NDC, T3), cpt(p_s[:, :, tof:tof + T3]),
                      waits=[tok_yg1])
        tok_yg = vemit(lambda e: e.tensor_mul(yg3[:, :], yg3[:, :], big2[:, :]),
                       waits=[tok_s])
        fr["big2"] = tok_yg
        if debug:
            b.dma("gpsimd", cpt(p_dy[:, :, tof:tof + T3]), r2(big2, NDC, T3),
                  waits=[tok_yg])
            b.dma("gpsimd", cpt(p_dyg[:, :, tof:tof + T3]), r2(yg3, NDC, T3),
                  waits=[tok_yg])
        fr["big1"] = tok_y
        tok_u = b.dma("sync", r2(u3, NDM, T3), u_v[:, :, tof:tof + T3],
                      waits=[tok_y, fr.get("xh3")])

        wov = r2(wout_sb, NDC, D)
        xev = {}
        for do in range(NDM):
            bk = do % 2
            w = [tok_yg, tok_wo3, xev.get(do - 2)]
            if do < 2:
                w.append(fr.get("mm3"))
            for kk in range(NDC):
                temit(lambda e, do=do, kk=kk, bk=bk: e.matmul(
                    bank(bk, T3), wov[:, kk, do * 128:(do + 1) * 128],
                    yg3[:, kk * T3:(kk + 1) * T3],
                    start=(kk == 0), stop=(kk == NDC - 1)),
                    waits=w if kk == 0 else ())
            tmm = b.tok("T")
            vemit(lambda e, do=do, bk=bk: e.scalar_tensor_tensor(
                xr3[:, do * T3:(do + 1) * T3], bank(bk, T3),
                bout_sb[:, do:do + 1], u3[:, do * T3:(do + 1) * T3],
                ALU.add, ALU.add),
                waits=[tmm, tok_u] + ([fr.get("xr")] if do == 0 else []))
            xev[do] = b.tok("V")
        tok_xr = b.tok("V")
        fr["mm3"] = tok_xr

        sq_t = None
        for c in range(NDM):
            sph = c % 2
            w = [tok_xr] + ([fr.get("rp")] if c == 0 else [])
            if sq_t is not None and c >= 2:
                w.append(sq_t)
            stok = semit(lambda e, c=c, sph=sph: e.activation(
                sq3a[:, sph * T3:(sph + 1) * T3], xr3[:, c * T3:(c + 1) * T3],
                ACT.Square), waits=w)
            temit(lambda e, c=c: e.matmul(bank(2)[0:1, 0:T3], ones_col[:, :],
                                          xr3[:, c * T3:(c + 1) * T3],
                                          start=(c == 0), stop=(c == NDM - 1)),
                  waits=[tok_xr] if c == 0 else ())
            temit(lambda e, c=c, sph=sph: e.matmul(
                bank(5)[0:1, 0:T3], ones_col[:, :], sq3a[:, sph * T3:(sph + 1) * T3],
                start=(c == 0), stop=(c == NDM - 1)), waits=[stok])
            sq_t = b.tok("T")
        tok_st = b.tok("T")
        rm = rp3[0:1, 0:T3]
        rv = rp3[0:1, T3:2 * T3]
        rq = rp3[0:1, 2 * T3:3 * T3]
        w = [tok_st] + ([fr.get("rp")] if "rp" in fr else [])
        vemit(lambda e: e.tensor_scalar_mul(rm, bank(2)[0:1, 0:T3], 1.0 / D), waits=w)
        vemit(lambda e: e.tensor_scalar_mul(rv, bank(5)[0:1, 0:T3], 1.0 / D))
        vemit(lambda e: e.tensor_mul(rq, rm, rm))
        tok_v3 = vemit(lambda e: e.tensor_sub(rv, rv, rq))
        semit(lambda e: e.activation(rv, rv, ACT.Ln, bias=eps_sb[0:1, :]),
              waits=[tok_v3])
        tok_rstd3 = semit(lambda e: e.activation(rv, rv, ACT.Exp, scale=-0.5))
        tok_mr3 = vemit(lambda e: e.tensor_mul(rm, rm, rv), waits=[tok_rstd3])
        temit(lambda e: e.matmul(bank(3, T3), ones_row[0:1, :], rv,
                                 start=True, stop=True), waits=[tok_mr3])
        tok_rep3 = temit(lambda e: e.matmul(bank(4, T3), ones_row[0:1, :], rm,
                                            start=True, stop=True))
        fr["rp"] = tok_rep3
        for c in range(NDM):
            vemit(lambda e, c=c: e.tensor_mul(xh3[:, c * T3:(c + 1) * T3],
                                              xr3[:, c * T3:(c + 1) * T3],
                                              bank(3, T3)),
                  waits=[tok_rep3] + ([fr.get("xh3w")] if c == 0 else []))
            vemit(lambda e, c=c: e.tensor_sub(xh3[:, c * T3:(c + 1) * T3],
                                              xh3[:, c * T3:(c + 1) * T3],
                                              bank(4, T3)))
        tok_xh3 = b.tok("V")

        tok_spxr = b.dma("gpsimd", cpt(p_xr2[:, :, tof:tof + T3]),
                         r2(xr3, NDM, T3), waits=[tok_xh3, fr.get("spxr")])
        tok_spxh = b.dma("gpsimd", cpt(p_xh2[:, :, tof:tof + T3]),
                         r2(xh3, NDM, T3), waits=[fr.get("spxh")])
        fr["spxr"] = tok_spxr
        fr["spxh"] = tok_spxh
        fr["xr"] = tok_spxr
        fr["yg"] = tok_spxr
        fr["xh3"] = tok_spxh
        fr["xh3w"] = tok_spxh
        fr["big1"] = tok_spxh
        fr["big2"] = tok_spxr

    ph3a_done = join_all()

    b.dma("sync", r2(wg_sb, 8, NE), wg_d.rearrange("(k p) f -> p k f", p=128),
          waits=ph3a_done)
    b.dma("sync", r2(acat_sb, 8, NE * NR),
          acat_d.rearrange("(k p) f -> p k f", p=128))
    tok_wcst = b.tok("LD")

    xh4_tok = [None, None]
    ff = {}

    def ld3b(i):
        slot = i % 2
        xh4_tok[slot] = b.dma("sync", r2(xh4[slot], NDM, T3),
                              cpt(p_xh2[:, :, i * T3:(i + 1) * T3]),
                              waits=[tok_wcst, ff.get(("xh4", slot))] +
                              (ph3a_done if i == 0 else []))

    ld3b(0)
    wic_tok = [None, None]
    woc_tok = [None, None]
    bcc_tok = [None, None]
    wgv = r2(wg_sb, 8, NE)
    acv = r2(acat_sb, 8, NE * NR)
    xr4_tok = None

    for i in range(N3):
        slot = i % 2
        if i + 1 < N3:
            ld3b(i + 1)
        xhv = xh4[slot]
        tof = i * T3

        for kk in range(NDM):
            temit(lambda e, kk=kk, xhv=xhv: e.matmul(
                bank(5, T3)[0:NE, :], wgv[:, kk, :], xhv[:, kk * T3:(kk + 1) * T3],
                start=(kk == 0), stop=(kk == NDM - 1)),
                waits=([xh4_tok[slot], tok_wcst, ff.get("b5")] if kk == 0 else ()))
        tok_lg = b.tok("T")
        tok_lev4 = semit(lambda e: e.activation(lgt4[0:NE, :], bank(5, T3)[0:NE, :],
                                                ACT.Copy),
                         waits=[tok_lg, ff.get("lgt4")])
        tok_lgr = b.dma("sync", lgr0.rearrange("o (n t) -> o n t", n=NE),
                        lgt4[0:NE, :], waits=[tok_lev4, ff.get("lgr")])
        ff["lgt4"] = tok_lgr
        tok_bgadd = vemit(lambda e: e.tensor_tensor(
            lgr0.rearrange("o (n t) -> o n t", n=NE),
            lgr0.rearrange("o (n t) -> o n t", n=NE),
            bg_sb[0:1, :].rearrange("o (n a) -> o n a", n=NE)
            .broadcast_to((1, NE, T3)), ALU.add),
            waits=[tok_lgr, ff.get("el")])
        tok_el = semit(lambda e: e.activation(expl0[:, :], lgr0[:, :], ACT.Exp),
                       waits=[tok_bgadd])

        def E(e_i):
            return expl0[:, e_i * T3:(e_i + 1) * T3]

        ta = ts0[:, 0:T3]
        tb = ts0[:, T3:2 * T3]
        tsum = ts0[:, 2 * T3:3 * T3]
        trinv = ts0[:, 3 * T3:4 * T3]
        w = [tok_el, ff.get("tsc")]
        vemit(lambda e: e.tensor_add(tsum, E(0), E(1)), waits=w)
        vemit(lambda e: e.tensor_add(ta, E(2), E(3)))
        vemit(lambda e: e.tensor_add(tsum, tsum, ta))
        vemit(lambda e: e.reciprocal(trinv, tsum))
        W4 = [w40[:, e_i * T3:(e_i + 1) * T3] for e_i in range(NE)]
        w = [ff.get("w40")]
        vemit(lambda e: e.tensor_tensor(W4[0], E(0), E(1), ALU.max), waits=w)
        vemit(lambda e: e.tensor_tensor(W4[1], E(0), E(1), ALU.min))
        vemit(lambda e: e.tensor_tensor(W4[2], E(2), E(3), ALU.max))
        vemit(lambda e: e.tensor_tensor(W4[3], E(2), E(3), ALU.min))
        vemit(lambda e: e.tensor_tensor(ta, W4[0], W4[2], ALU.min))
        vemit(lambda e: e.tensor_tensor(tb, W4[1], W4[3], ALU.max))
        tok_m2 = vemit(lambda e: e.tensor_tensor(ta, ta, tb, ALU.max))
        vemit(lambda e: e.tensor_tensor(
            w40[:, :].rearrange("o (n t) -> o n t", n=NE),
            expl0[:, :].rearrange("o (n t) -> o n t", n=NE),
            ta.rearrange("o (a t) -> o a t", a=1).broadcast_to((1, NE, T3)),
            ALU.is_ge))
        vemit(lambda e: e.tensor_mul(w40[:, :], w40[:, :], expl0[:, :]))
        tok_w4 = vemit(lambda e: e.tensor_tensor(
            w40[:, :].rearrange("o (n t) -> o n t", n=NE),
            w40[:, :].rearrange("o (n t) -> o n t", n=NE),
            trinv.rearrange("o (a t) -> o a t", a=1).broadcast_to((1, NE, T3)),
            ALU.mult))
        ff["el"] = tok_w4
        ff["tsc"] = tok_w4
        ff["lgr"] = tok_w4
        ff["bank7"] = tok_w4

        wev = []
        for e_i in range(NE):
            bk = e_i % 2
            w = [tok_w4]
            if len(wev) >= 2:
                w.append(wev[-2])
            elif ff.get("b01") is not None:
                w.append(ff.get("b01"))
            temit(lambda e, e_i=e_i, bk=bk: e.matmul(
                bank(bk, T3), ones_row[0:1, :], w40[:, e_i * T3:(e_i + 1) * T3],
                start=True, stop=True), waits=w)
            tmm = b.tok("T")
            semit(lambda e, e_i=e_i, bk=bk: e.activation(
                wrep[:, e_i * T3:(e_i + 1) * T3], bank(bk, T3), ACT.Copy),
                waits=[tmm])
            wev.append(b.tok("S"))
        tok_wrep = b.tok("S")
        ff["w40"] = tok_wrep

        for kk in range(NDM):
            temit(lambda e, kk=kk, xhv=xhv: e.matmul(
                bank(5, T3)[0:NE * NR, :], acv[:, kk, :],
                xhv[:, kk * T3:(kk + 1) * T3],
                start=(kk == 0), stop=(kk == NDM - 1)),
                waits=([tok_el] if kk == 0 else ()))
        tok_llm = b.tok("T")
        ff["b5"] = tok_llm
        tok_ll = semit(lambda e: e.activation(llsb16[0:NE * NR, :],
                                              bank(5, T3)[0:NE * NR, :], ACT.Copy),
                       waits=[tok_llm, ff.get("llsb")])

        gdone = []
        hp_done = []
        lev_done = []
        for fc in range(NFC):
            ws = fc % 2
            bb = 2 if fc % 2 == 0 else 7
            wic_tok[ws] = b.dma("sync", r2(wic[ws], 8, 128),
                                wi_d[:, fc * 128:(fc + 1) * 128]
                                .rearrange("(k p) f -> p k f", p=128),
                                waits=[wic_tok[ws],
                                       gdone[-2] if len(gdone) >= 2 else None] +
                                ([tok_wcst] if fc < 2 else []))
            bcc_tok[ws] = b.dma("sync", bcatc[ws][0:16, :],
                                bcat_d[:, fc * NE * 128:(fc + 1) * NE * 128],
                                waits=[bcc_tok[ws],
                                       gdone[-2] if len(gdone) >= 2 else None])
            for kk in range(NDM):
                w = ()
                if kk == 0:
                    w = [wic_tok[ws]]
                    if len(hp_done) >= 2:
                        w.append(hp_done[-2])
                    if fc == 0:
                        w.append(ff.get("bank7"))
                temit(lambda e, kk=kk, ws=ws, xhv=xhv, bb=bb: e.matmul(
                    bank(bb, T3), wic[ws][:, kk * 128:(kk + 1) * 128],
                    xhv[:, kk * T3:(kk + 1) * T3],
                    start=(kk == 0), stop=(kk == NDM - 1)), waits=w)
            tok_base = b.tok("T")
            for e_i in range(NE):
                w = ()
                if e_i == 0:
                    w = [bcc_tok[ws], tok_ll, tok_w4]
                    if lev_done:
                        w.append(lev_done[-1])
                temit(lambda e, e_i=e_i, ws=ws: e.matmul(
                    bank(3 + e_i, T3), bcatc[ws][0:16, e_i * 128:(e_i + 1) * 128],
                    llsb16[0:16, :], start=True, stop=True), waits=w)
            tok_lora = b.tok("T")
            semit(lambda e, ws=ws: e.activation(
                lor4[ws][:, :], ps[:, 3 * 512:3 * 512 + NE * T3], ACT.Copy),
                waits=[tok_lora] + ([gdone[-2]] if len(gdone) >= 2 else []))
            tok_lev = b.tok("S")
            lev_done.append(tok_lev)
            vemit(lambda e, ws=ws, bb=bb: e.scalar_tensor_tensor(
                r2(hp4, NE, T3),
                bank(bb, T3).rearrange("p (a t) -> p a t", a=1)
                .broadcast_to((128, NE, T3)),
                1.0, r2(lor4[ws], NE, T3), ALU.mult, ALU.add),
                waits=[tok_base, tok_lev] +
                ([ff.get("outd")] if fc == 0 else []))
            tok_hp = b.tok("V")
            hp_done.append(tok_hp)
            vemit(lambda e: e.tensor_scalar_max(hr4[:, :], hp4[:, :], 0.0))
            vemit(lambda e: e.tensor_tensor(
                hw4[:, :].rearrange("p (t n) -> p n t", n=NE),
                r2(hr4, NE, T3), r2(wrep, NE, T3), ALU.mult),
                waits=[tok_wrep])
            vemit(lambda e, fc=fc: e.tensor_reduce(
                g4[:, fc * T3:(fc + 1) * T3],
                hw4[:, :].rearrange("p (t n) -> p t n", n=NE), AX.X, ALU.add))
            gdone.append(b.tok("V"))
        tok_g = b.tok("V")
        ff["llsb"] = tok_g

        if debug:
            b.dma("gpsimd", cpt(p_dg[:, :, tof:tof + T3]), r2(g4, NFC, T3),
                  waits=[tok_g])
            b.dma("gpsimd", p_dw[0:1, :].rearrange("o (n t) -> o n t", n=NE)
                  [:, :, tof:tof + T3],
                  w40.rearrange("o (n t) -> o n t", n=NE), waits=[tok_g])
            b.dma("gpsimd", p_dll[:, tof:tof + T3], llsb16[0:16, :], waits=[tok_g])
            b.dma("gpsimd", cpt(p_dbase[:, :, tof:tof + T3]), r2(hp4, NE, T3),
                  waits=[tok_g])
        xr4_tok = b.dma("sync", r2(xr4, NDM, T3), cpt(p_xr2[:, :, tof:tof + T3]),
                        waits=[ff.get("xr4")])

        wo_mm = {}
        for kk in range(NFC):
            ws = kk % 2
            woc_tok[ws] = b.dma("sync", woc[ws][:, :],
                                wo_d[kk * 128:(kk + 1) * 128, :],
                                waits=[woc_tok[ws], tok_g, wo_mm.get(kk - 2)])
            for do in range(NDM):
                w = ()
                if do == 0:
                    w = [woc_tok[ws], tok_g]
                    if kk == 0:
                        w.append(ff.get("out"))
                temit(lambda e, kk=kk, do=do, ws=ws: e.matmul(
                    bank(do, T3), woc[ws][:, do * 128:(do + 1) * 128],
                    g4[:, kk * T3:(kk + 1) * T3],
                    start=(kk == 0), stop=(kk == NFC - 1)), waits=w)
            wo_mm[kk] = b.tok("T")
        tok_wo = b.tok("T")
        for do in range(NDM):
            vemit(lambda e, do=do: e.tensor_add(
                out4[:, do * T3:(do + 1) * T3], bank(do, T3),
                xr4[:, do * T3:(do + 1) * T3]),
                waits=[tok_wo, xr4_tok, ff.get("outd")] if do == 0 else ())
        tok_out = b.tok("V")
        ff[("xh4", slot)] = tok_g
        ff["xr4"] = tok_out
        ff["b5"] = tok_out
        ff["b01"] = tok_out
        ff["out"] = b.dma("sync", out_v[:, :, tof:tof + T3], r2(out4, NDM, T3),
                          waits=[tok_out])
        ff["outd"] = ff["out"]

    if debug:
        allj = join_all()
        dbg_car = nc.dram_tensor("d_car", [128, NDC * NMY], F32,
                                 kind="ExternalOutput").ap()
        b.dma("sync", dbg_car[:, :], car_sb[:, :], waits=allj)
        dbg = {
            "d_delta": (sh_delta, [2, NDC, 128, TH]),
            "d_deltax": (sh_deltax, [2, NDC, 128, TH]),
            "d_bc": (sh_bc, [2, 32, TH]),
            "d_yp": (p_yp, [NDC, 128, TH]),
            "d_shyp": (sh_yp, [2, NDC, 128, TH]),
            "d_s": (p_s, [NDC, 128, TH]),
            "d_g1": (p_g1, [NDC, 128, TH]),
            "d_rs": (p_rs, [1, 2 * TH]),
            "d_xr2": (p_xr2, [NDM, 128, TH]),
            "d_dy": (p_dy, [NDC, 128, TH]),
            "d_dg": (p_dg, [NFC, 128, TH]),
            "d_dw": (p_dw, [1, NE * TH]),
            "d_dll": (p_dll, [16, TH]),
            "d_dbase": (p_dbase, [NE, 128, TH]),
            "d_dyg": (p_dyg, [NDC, 128, TH]),
            "d_xh2": (p_xh2, [NDM, 128, TH]),
        }
        for nm, (src_ap, shp) in dbg.items():
            od = nc.dram_tensor(nm, shp, F32, kind="ExternalOutput").ap()
            b.dma("sync", od[tuple(slice(None) for _ in shp)], src_ap[
                tuple(slice(None) for _ in shp)], waits=allj)

    final_ld = b.tok("LD")
    final_gd = b.tok("GD")
    b.emit("sync", lambda e: None, waits=[final_ld], inc=None)
    b.emit("gpsimd", lambda e: None, waits=[final_gd], inc=None)

    with (
        nc.semaphore("sLD") as s_ld,
        nc.semaphore("sGD") as s_gd,
        nc.semaphore("sT") as s_t,
        nc.semaphore("sS") as s_s,
        nc.semaphore("sV") as s_v,
        nc.semaphore("sBAR") as s_bar,
        nc.Block() as block,
    ):
        b.sems.update({"LD": s_ld, "GD": s_gd, "T": s_t, "S": s_s, "V": s_v,
                       "BAR": s_bar})
        b.regs.update({"sync": nc.sync.alloc_register("par_sync"),
                       "gpsimd": nc.gpsimd.alloc_register("par_gp")})

        def mk(eng):
            def body(e):
                for fn, waits, inc, amt in b.q[eng]:
                    for (sn, val) in waits:
                        e.wait_ge(b.sems[sn], val)
                    ins = fn(e)
                    if inc is not None and ins is not None:
                        ins.then_inc(b.sems[inc], amt)
            return body

        block.sync(mk("sync"))
        block.scalar(mk("scalar"))
        block.vector(mk("vector"))
        block.tensor(mk("tensor"))
        block.gpsimd(mk("gpsimd"))

    return nc


_CACHE = {}


def _get_nc(debug=False):
    key = "ncd" if debug else "nc"
    if key not in _CACHE:
        _CACHE[key] = build_program(debug)
    return _CACHE[key]


def _prep_inputs(inputs):
    def c(x):
        return np.ascontiguousarray(np.asarray(x), dtype=np.float32)

    u = c(inputs["L0_attn_u"])
    W_in = c(inputs["L0_attn_W_in"])
    b_in = c(inputs["L0_attn_b_in"])
    W_x = c(inputs["L0_attn_W_x"])
    b_x = c(inputs["L0_attn_b_x"])
    W_dt = c(inputs["L0_attn_W_dt"])
    b_dt = c(inputs["L0_attn_b_dt"])
    A_log = c(inputs["L0_attn_A_log"])
    Dv = c(inputs["L0_attn_D"])
    W_out = c(inputs["L0_attn_W_out"])
    b_out = c(inputs["L0_attn_b_out"])
    Wg = c(inputs["L0_ffn_Wg"])
    bg = c(inputs["L0_ffn_bg"])
    Wi = c(inputs["L0_ffn_Wi"])
    Wo = c(inputs["L0_ffn_Wo"])
    Af = c(inputs["L0_ffn_A"])
    Bf = c(inputs["L0_ffn_B"])

    bcat = np.zeros((16, NFC, NE, 128), np.float32)
    for e_i in range(NE):
        bcat[e_i * NR:(e_i + 1) * NR, :, e_i, :] = (
            Bf[e_i].reshape(NFC, 128, NR).transpose(2, 0, 1))
    bcat = bcat.reshape(16, NFC * NE * 128)

    shared = {
        "winx": np.ascontiguousarray(W_in[:DI].T),
        "winz": np.ascontiguousarray(W_in[DI:].T),
        "binx": np.ascontiguousarray(b_in[:DI].reshape(NDC, 128).T),
        "binz": np.ascontiguousarray(b_in[DI:].reshape(NDC, 128).T),
        "wx": np.ascontiguousarray(W_x.T),
        "bx": np.ascontiguousarray(b_x.reshape(96, 1)),
        "wdt": np.ascontiguousarray(W_dt.T),
        "bdt": np.ascontiguousarray(b_dt.reshape(NDC, 128).T),
        "dvec": np.ascontiguousarray(Dv.reshape(NDC, 128).T),
        "wout": np.ascontiguousarray(W_out.T),
        "bout": np.ascontiguousarray(b_out.reshape(NDM, 128).T),
        "wg": np.ascontiguousarray(Wg.T),
        "bg": np.ascontiguousarray(bg.reshape(1, NE)),
        "wi": np.ascontiguousarray(Wi.T),
        "acat": np.ascontiguousarray(Af.reshape(NE * NR, D).T),
        "bcat": np.ascontiguousarray(bcat),
        "wo": np.ascontiguousarray(Wo.T),
    }

    in_maps = []
    for c_id in range(8):
        bidx, q = c_id // 2, c_id % 2
        m = dict(shared)
        m["u"] = np.ascontiguousarray(u[bidx, q * TH:(q + 1) * TH, :].T)
        m["par"] = np.full((1, 1), q, np.int32)
        ns = list(range(q * NMY, (q + 1) * NMY))
        amy = A_log[:, ns].reshape(NDC, 128, NMY).transpose(1, 0, 2)
        m["amy"] = np.ascontiguousarray(amy.reshape(128, NDC * NMY))
        sel = np.zeros((32, 2 * NMY * 128), np.float32)
        for jn, gn in enumerate(ns):
            sel[gn, jn * 128:(jn + 1) * 128] = 1.0
            sel[16 + gn, (NMY + jn) * 128:(NMY + jn + 1) * 128] = 1.0
        m["sel"] = sel
        in_maps.append(m)
    return in_maps


def kernel(**inputs):
    nc = _get_nc()
    in_maps = _prep_inputs(inputs)
    res = run_bass_kernel_spmd(nc, in_maps, core_ids=list(range(8)))
    out = np.empty((4, TF, D), np.float32)
    for c_id in range(8):
        bidx, q = c_id // 2, c_id % 2
        out[bidx, q * TH:(q + 1) * TH, :] = res.results[c_id]["out"].T
    return out


# version 06
# speedup vs baseline: 1.1783x; 1.0085x over previous
import numpy as np
import concourse.bass as bass
import concourse.mybir as mybir
from concourse.bass_utils import run_bass_kernel_spmd

F32 = mybir.dt.float32
I32 = mybir.dt.int32
ALU = mybir.AluOpType
ACT = mybir.ActivationFunctionType
AX = mybir.AxisListType

D = 1024
DI = 2048
DTR = 64
NS = 16
NMY = 8
TH = 4096
TF = 8192
FF = 4096
NE = 4
NR = 4

T1X = 256
N1X = TH // T1X
T1Z = 512
N1Z = TH // T1Z
T2 = 512
N2 = TF // T2
T3 = 512
N3 = TH // T3

NDC = DI // 128
NDM = D // 128
NFC = FF // 128

ENGS = ("sync", "scalar", "vector", "tensor", "gpsimd")
PAIRS = [[0, 1], [2, 3], [4, 5], [6, 7]]


class Build:
    def __init__(self, nc, semnames):
        self.nc = nc
        self.q = {e: [] for e in ENGS}
        self.cnt = {s: 0 for s in semnames}
        self.sems = {}
        self.regs = {}

    def emit(self, eng, fn, waits=(), inc=None, amt=1):
        ws = tuple(w for w in waits if w is not None)
        self.q[eng].append((fn, ws, inc, amt))
        if inc is not None:
            self.cnt[inc] += amt
            return (inc, self.cnt[inc])
        return None

    def dma(self, eng, out, in_, waits=(), sem=None):
        sem = sem or ("GD" if eng == "gpsimd" else "LD")

        def fn(e, out=out, in_=in_):
            return e.dma_start(out=out, in_=in_)
        return self.emit(eng, fn, waits, sem, 16)

    def dma_if(self, eng, out0, in0, out1, in1, waits=(), sem=None):
        sem = sem or ("GD" if eng == "gpsimd" else "LD")
        semn, regs, sems = sem, self.regs, self.sems

        def fn(e, out0=out0, in0=in0, out1=out1, in1=in1):
            with e.If_eq(regs[eng], 0):
                e.dma_start(out=out0, in_=in0).then_inc(sems[semn], 16)
            with e.Else():
                e.dma_start(out=out1, in_=in1).then_inc(sems[semn], 16)
            return None
        return self.emit(eng, fn, waits, semn, 16)

    def tok(self, sem):
        return (sem, self.cnt[sem])


def build_program(debug=False):
    nc = bass.Bass("TRN2", target_bir_lowering=False)

    def din(name, shape, dt=F32):
        return nc.dram_tensor(name, list(shape), dt, kind="ExternalInput").ap()

    u_d = din("u", [D, TH])
    par_d = din("par", [1, 1], I32)
    winx_d = din("winx", [D, DI])
    winz_d = din("winz", [D, DI])
    binx_d = din("binx", [128, NDC])
    binz_d = din("binz", [128, NDC])
    wx_d = din("wx", [DI, 96])
    bx_d = din("bx", [96, 1])
    wdt_d = din("wdt", [DTR, DI])
    bdt_d = din("bdt", [128, NDC])
    amy_d = din("amy", [128, NDC * NMY])
    dvec_d = din("dvec", [128, NDC])
    sel_d = din("sel", [32, 2 * NMY * 128])
    wout_d = din("wout", [DI, D])
    bout_d = din("bout", [128, NDM])
    wg_d = din("wg", [D, NE])
    bg_d = din("bg", [1, NE])
    wi_d = din("wi", [D, FF])
    acat_d = din("acat", [D, NE * NR])
    bcat_d = din("bcat", [16, NFC * NE * 128])
    wo_d = din("wo", [FF, D])

    out_d = nc.dram_tensor("out", [D, TH], F32, kind="ExternalOutput").ap()
    out_v = out_d.rearrange("(c p) t -> p c t", p=128)

    def shr(name, shape):
        return nc.dram_tensor(name, list(shape), F32, kind="Internal",
                              addr_space="Shared").ap()

    def pri(name, shape):
        return nc.dram_tensor(name, list(shape), F32, kind="Internal").ap()

    sh_delta = shr("sh_delta", [2, NDC, 128, TH])
    sh_deltax = shr("sh_deltax", [2, NDC, 128, TH])
    sh_bc = shr("sh_bc", [2, 32, TH])
    sh_yp = shr("sh_yp", [2, NDC, 128, TH])
    p_yp = pri("p_yp", [NDC, 128, TH])
    p_s = pri("p_s", [NDC, 128, TH])
    p_g1 = pri("p_g1", [NDC, 128, TH])
    p_rs = pri("p_rs", [1, 2 * TH])
    p_xr2 = pri("p_xr2", [NDM, 128, TH])
    p_xh2 = pri("p_xh2", [NDM, 128, TH])
    p_dy = pri("p_dy", [NDC, 128, TH])
    p_dg = pri("p_dg", [NFC, 128, TH])
    p_dw = pri("p_dw", [1, NE * TH])
    p_dll = pri("p_dll", [16, TH])
    p_dbase = pri("p_dbase", [NE, 128, TH])
    p_dyg = pri("p_dyg", [NDC, 128, TH])
    bar_i = pri("bar_i", [1, 1])
    bar_o = pri("bar_o", [1, 1])

    ARENA_COLS = 52000
    arena = nc.alloc_sbuf_tensor("arena", [128, ARENA_COLS], F32).ap()
    par_sb = nc.alloc_sbuf_tensor("par_sb", [1, 1], I32).ap()

    class Alloc:
        def __init__(self, off=0):
            self.off = off

        def a(self, cols):
            assert self.off + cols <= ARENA_COLS, (self.off, cols)
            v = arena[:, self.off:self.off + cols]
            self.off += cols
            return v

    AL = Alloc()
    ones_col = AL.a(1)
    ones_row = AL.a(128)
    binx_sb = AL.a(NDC)
    binz_sb = AL.a(NDC)
    bx_sb = AL.a(1)
    bdt_sb = AL.a(NDC)
    amy_sb = AL.a(NDC * NMY)
    dvec_sb = AL.a(NDC)
    bout_sb = AL.a(NDM)
    bg_sb = AL.a(NE)
    eps_sb = AL.a(1)
    car_sb = AL.a(NDC * NMY)
    PERM = AL.off

    A = Alloc(PERM)
    winx_sb = A.a(8 * DI)
    wx_sb = A.a(NDC * 96)
    wdt_sb = A.a(DI)
    u1 = [A.a(8 * T1X) for _ in range(2)]
    usq1 = A.a(2 * T1X)
    dbcb = A.a(T1X)
    xh1 = [A.a(8 * T1X) for _ in range(2)]
    xsb = A.a(NDC * T1X)
    delt1 = A.a(NDC * T1X)
    dtmp1 = A.a(T1X)
    lntmp = A.a(3 * T1X)
    g11 = A.a(NDC * T1X)
    smreg = A.a(2 * TH)
    assert A.off <= ARENA_COLS, A.off
    rsacc = smreg[0:1, :]
    bcacc = smreg[32:64, 0:TH]

    A = Alloc(PERM)
    winz_sb = A.a(8 * DI)
    u2 = [A.a(8 * T1Z) for _ in range(2)]
    xh1z = A.a(8 * T1Z)
    rs2 = A.a(2 * T1Z)
    ssb = A.a(NDC * T1Z)
    assert A.off <= ARENA_COLS

    A = Alloc(PERM)
    sel_sb = A.a(2 * NMY * 128)
    bct = A.a(T2)
    dlt2 = [A.a(T2) for _ in range(4)]
    dxt2 = [A.a(T2) for _ in range(4)]
    brep = A.a(NMY * T2)
    crep = A.a(NMY * T2)
    da2 = [A.a(NMY * T2) for _ in range(2)]
    b2 = A.a(NMY * T2)
    h2 = A.a(NMY * T2)
    hc2 = A.a(NMY * T2)
    y2 = A.a(NDC * T2)
    assert A.off <= ARENA_COLS

    A = Alloc(PERM)
    wout_sb = A.a(NDC * D)
    big1 = A.a(NDC * T3)
    big2 = A.a(NDC * T3)
    yg3 = A.a(NDC * T3)
    sq3a = A.a(2 * T3)
    rp3 = A.a(4 * T3)
    assert A.off <= ARENA_COLS
    xh3 = big1[:, 0:NDM * T3]
    u3 = big1[:, NDM * T3:2 * NDM * T3]
    xr3 = big2[:, 0:NDM * T3]

    A = Alloc(PERM)
    wg_sb = A.a(8 * NE)
    acat_sb = A.a(8 * NE * NR)
    bcatc = [A.a(NE * 128) for _ in range(2)]
    xh4 = [A.a(NDM * T3) for _ in range(2)]
    xr4 = A.a(NDM * T3)
    wic = [A.a(8 * 128) for _ in range(2)]
    woc = [A.a(D) for _ in range(2)]
    llsb16 = A.a(T3)
    lgt4 = A.a(T3)
    row0 = A.a(3 * NE * T3)
    wrep = A.a(NE * T3)
    lor4 = [A.a(NE * T3) for _ in range(2)]
    hpr = A.a(2 * NE * T3)
    hp4 = hpr[:, 0:NE * T3]
    hr4 = hpr[:, NE * T3:2 * NE * T3]
    out4 = hpr
    hw4 = hp4
    g4 = A.a(NFC * T3)
    assert A.off <= ARENA_COLS, A.off
    expl0 = row0[0:1, 0:NE * T3]
    w40 = row0[0:1, NE * T3:2 * NE * T3]
    ts0 = row0[0:1, 2 * NE * T3:3 * NE * T3]
    lgr0 = ts0

    ps = nc.alloc_psum_tensor("ps", [128, 4096], F32).ap()

    def bank(i, cols=512):
        return ps[:, i * 512:i * 512 + cols]

    SEMS = ("LD", "GD", "T", "S", "V", "BAR")
    b = Build(nc, SEMS)

    def r2(ap, c, t):
        return ap.rearrange("p (c t) -> p c t", c=c)

    def cpt(dram3):
        return dram3.rearrange("c p t -> p c t")

    def vemit(fn, waits=()):
        return b.emit("vector", fn, waits, "V")

    def semit(fn, waits=()):
        return b.emit("scalar", fn, waits, "S")

    def temit(fn, waits=()):
        return b.emit("tensor", fn, waits, "T")

    tok_par = b.dma("sync", par_sb[:, :], par_d[:, :])
    b.dma("sync", binx_sb[:, :], binx_d[:, :])
    b.dma("sync", binz_sb[:, :], binz_d[:, :])
    b.dma("sync", bx_sb[0:96, :], bx_d[:, :])
    b.dma("sync", bdt_sb[:, :], bdt_d[:, :])
    b.dma("sync", amy_sb[:, :], amy_d[:, :])
    b.dma("sync", dvec_sb[:, :], dvec_d[:, :])
    b.dma("sync", bout_sb[:, :], bout_d[:, :])
    b.dma("sync", bg_sb[0:1, :], bg_d[:, :])
    tok_consts = b.tok("LD")

    vemit(lambda e: e.memset(ones_col[:, :], 1.0), waits=[tok_consts])
    vemit(lambda e: e.memset(ones_row[0:1, :], 1.0))
    vemit(lambda e: e.memset(car_sb[:, :], 0.0))
    vemit(lambda e: e.memset(eps_sb[:, :], 1e-5))
    tok_ones = b.tok("V")
    tok_aexp = semit(lambda e: e.activation(amy_sb[:, :], amy_sb[:, :], ACT.Exp),
                     waits=[tok_consts])
    tok_amy = vemit(lambda e: e.tensor_scalar_mul(amy_sb[:, :], amy_sb[:, :], -1.0),
                    waits=[tok_aexp])

    for eng in ("sync", "gpsimd"):
        def fn(e, eng=eng):
            e.reg_load(b.regs[eng], par_sb[0:1, 0:1])
            return None
        b.emit(eng, fn, waits=[tok_par], inc=None)

    u_v = u_d.rearrange("(c p) t -> p c t", p=128)

    def join_all():
        return [b.tok("T"), b.tok("S"), b.tok("V"), b.tok("GD")]

    b.dma("sync", r2(winx_sb, 8, DI), winx_d.rearrange("(k p) f -> p k f", p=128))
    b.dma("sync", r2(wx_sb, NDC, 96), wx_d.rearrange("(k p) f -> p k f", p=128))
    b.dma("sync", wdt_sb[0:DTR, :].rearrange("p (c f) -> p c f", c=NDC),
          wdt_d.rearrange("k (c f) -> k c f", c=NDC))
    tok_w1x = b.tok("LD")

    u1_tok = [None, None]
    u1_free = [None, None]

    def ph1x_load(k):
        slot = k % 2
        u1_tok[slot] = b.dma("sync", r2(u1[slot], 8, T1X),
                             u_v[:, :, k * T1X:(k + 1) * T1X],
                             waits=[u1_free[slot]])

    ph1x_load(0)
    jtok = []
    spill_d = spill_dx = spill_g1 = None
    xh_tok = [None, None]
    stats_free = None
    sq_free = None

    def ph1x_ln(k):
        slot = k % 2
        ut = r2(u1[slot], 8, T1X)
        base_w = [u1_tok[slot], tok_w1x, tok_ones, tok_amy]
        sqmm = {}
        for c in range(8):
            ph = c % 2
            tok_sq = semit(lambda e, c=c, ph=ph, slot=slot: e.activation(
                usq1[:, ph * T1X:(ph + 1) * T1X],
                u1[slot][:, c * T1X:(c + 1) * T1X], ACT.Square),
                waits=(base_w + [sq_free]) if c == 0 else
                ([sqmm[c - 2]] if c >= 2 else ()))
            temit(lambda e, c=c, ut=ut: e.matmul(bank(2)[0:1, 0:T1X], ones_col[:, :],
                                                 ut[:, c, :], start=(c == 0),
                                                 stop=(c == 7)),
                  waits=(base_w + [stats_free]) if c == 0 else ())
            temit(lambda e, c=c, ph=ph: e.matmul(
                bank(5)[0:1, 0:T1X], ones_col[:, :],
                usq1[:, ph * T1X:(ph + 1) * T1X],
                start=(c == 0), stop=(c == 7)), waits=[tok_sq])
            sqmm[c] = b.tok("T")
        tok_stats = b.tok("T")
        rsl_r = rsacc[0:1, k * T1X:(k + 1) * T1X]
        rsl_m = rsacc[0:1, TH + k * T1X:TH + (k + 1) * T1X]
        lm = lntmp[0:1, 0:T1X]
        lv = lntmp[0:1, T1X:2 * T1X]
        lq = lntmp[0:1, 2 * T1X:3 * T1X]
        vemit(lambda e, lm=lm: e.tensor_scalar_mul(lm, bank(2)[0:1, 0:T1X],
                                                   1.0 / D), waits=[tok_stats])
        vemit(lambda e, lv=lv: e.tensor_scalar_mul(lv, bank(5)[0:1, 0:T1X],
                                                   1.0 / D))
        vemit(lambda e, lm=lm, lq=lq: e.tensor_mul(lq, lm, lm))
        tok_var = vemit(lambda e, lv=lv, lq=lq: e.tensor_sub(lv, lv, lq))
        semit(lambda e, lv=lv: e.activation(lv, lv, ACT.Ln,
                                            bias=eps_sb[0:1, :]), waits=[tok_var])
        tok_rstd = semit(lambda e, rsl_r=rsl_r, lv=lv: e.activation(
            rsl_r, lv, ACT.Exp, scale=-0.5))
        tok_mr = vemit(lambda e, rsl_r=rsl_r, rsl_m=rsl_m, lm=lm: e.tensor_mul(
            rsl_m, lm, rsl_r), waits=[tok_rstd])
        return slot, ut, tok_mr

    def ph1x_rep_xh(k, st):
        slot, ut, tok_mr = st
        temit(lambda e: e.matmul(bank(3, T1X), ones_row[0:1, :],
                                 rsacc[0:1, k * T1X:(k + 1) * T1X],
                                 start=True, stop=True), waits=[tok_mr])
        tok_rep = temit(lambda e: e.matmul(
            bank(4, T1X), ones_row[0:1, :],
            rsacc[0:1, TH + k * T1X:TH + (k + 1) * T1X], start=True, stop=True))
        xh = xh1[slot]
        for c in range(8):
            vemit(lambda e, c=c, ut=ut, xh=xh: e.tensor_mul(
                xh[:, c * T1X:(c + 1) * T1X], ut[:, c, :], bank(3, T1X)),
                waits=[tok_rep, xh_tok[slot]] if c == 0 else ())
            vemit(lambda e, c=c, xh=xh: e.tensor_sub(
                xh[:, c * T1X:(c + 1) * T1X], xh[:, c * T1X:(c + 1) * T1X],
                bank(4, T1X)))
        tok_xh = b.tok("V")
        u1_free[slot] = tok_xh
        xh_tok[slot] = tok_xh
        return tok_rep, tok_xh

    st = ph1x_ln(0)
    _, _ = None, None
    rep0, xh0 = ph1x_rep_xh(0, st)
    pend = {0: (rep0, xh0)}

    for k in range(N1X):
        slot = k % 2
        if k + 1 < N1X:
            ph1x_load(k + 1)
        tok_rep_k, tok_xh = pend.pop(k)
        xh = xh1[slot]

        xw = r2(winx_sb, 8, DI)
        ev = {}
        for oc in range(NDC):
            bk = oc % 2
            w = [tok_xh, ev.get(oc - 2)] + (jtok if oc == 0 else [])
            for kk in range(8):
                temit(lambda e, oc=oc, kk=kk, bk=bk, xh=xh: e.matmul(
                    bank(bk, T1X), xw[:, kk, oc * 128:(oc + 1) * 128],
                    xh[:, kk * T1X:(kk + 1) * T1X],
                    start=(kk == 0), stop=(kk == 7)),
                    waits=w if kk == 0 else ())
            tmm = b.tok("T")
            semit(lambda e, oc=oc, bk=bk: e.activation(
                xsb[:, oc * T1X:(oc + 1) * T1X], bank(bk, T1X), ACT.Identity,
                bias=binx_sb[:, oc:oc + 1]), waits=[tmm] + (jtok if oc == 0 else []))
            semit(lambda e, oc=oc: e.activation(
                g11[:, oc * T1X:(oc + 1) * T1X], xsb[:, oc * T1X:(oc + 1) * T1X],
                ACT.Copy, scale=dvec_sb[:, oc:oc + 1]))
            ev[oc] = b.tok("S")
        tok_xall = b.tok("S")

        if k + 1 < N1X:
            stats_free = None
            st_next = ph1x_ln(k + 1)

        wxv = r2(wx_sb, NDC, 96)
        for kk in range(NDC):
            temit(lambda e, kk=kk: e.matmul(
                bank(6, T1X)[0:96, :], wxv[:, kk, :],
                xsb[:, kk * T1X:(kk + 1) * T1X],
                start=(kk == 0), stop=(kk == NDC - 1)),
                waits=[tok_xall] if kk == 0 else ())
        tok_wx = b.tok("T")
        tok_dbc = semit(lambda e: e.activation(dbcb[0:96, :], bank(6, T1X)[0:96, :],
                                               ACT.Identity, bias=bx_sb[0:96, :]),
                        waits=[tok_wx])
        tok_bccp = vemit(lambda e, k=k: e.tensor_copy(
            bcacc[:, k * T1X:(k + 1) * T1X], dbcb[64:96, :]), waits=[tok_dbc])

        wdv = wdt_sb[0:DTR, :].rearrange("p (c f) -> p c f", c=NDC)
        dev = {}
        for oc in range(NDC):
            bk = oc % 2
            w = [tok_dbc, dev.get(oc - 2)]
            if oc < 2:
                w.append(ev[NDC - 2 + oc])
            temit(lambda e, oc=oc, bk=bk: e.matmul(
                bank(bk, T1X), wdv[:, oc, :], dbcb[0:DTR, :],
                start=True, stop=True), waits=w)
            tmm = b.tok("T")
            semit(lambda e, oc=oc, bk=bk: e.activation(
                dtmp1[:, :], bank(bk, T1X), ACT.Exp, bias=bdt_sb[:, oc:oc + 1]),
                waits=[tmm])
            semit(lambda e, oc=oc: e.activation(
                delt1[:, oc * T1X:(oc + 1) * T1X], dtmp1[:, :], ACT.Ln, bias=1.0))
            dev[oc] = b.tok("S")
        tok_delta = b.tok("S")

        if k + 1 < N1X:
            pend[k + 1] = ph1x_rep_xh(k + 1, st_next)
            sq_free = b.tok("T")

        tok_tend = b.tok("T")

        tok_dx = vemit(lambda e: e.tensor_mul(xsb[:, :], delt1[:, :], xsb[:, :]),
                       waits=[tok_delta, tok_xall])

        spill_d = b.dma_if(
            "gpsimd",
            cpt(sh_delta[0, :, :, k * T1X:(k + 1) * T1X]), r2(delt1, NDC, T1X),
            cpt(sh_delta[1, :, :, k * T1X:(k + 1) * T1X]), r2(delt1, NDC, T1X),
            waits=[tok_delta, spill_d])
        spill_dx = b.dma_if(
            "gpsimd",
            cpt(sh_deltax[0, :, :, k * T1X:(k + 1) * T1X]), r2(xsb, NDC, T1X),
            cpt(sh_deltax[1, :, :, k * T1X:(k + 1) * T1X]), r2(xsb, NDC, T1X),
            waits=[tok_dx, spill_dx])
        spill_g1 = b.dma("gpsimd", cpt(p_g1[:, :, k * T1X:(k + 1) * T1X]),
                         r2(g11, NDC, T1X), waits=[tok_xall, spill_g1])
        jtok = [spill_d, spill_dx, spill_g1, tok_dx, tok_bccp, tok_delta, tok_tend]

    b.dma_if("gpsimd", sh_bc[0, :, :], bcacc[:, :],
             sh_bc[1, :, :], bcacc[:, :], waits=[b.tok("V"), b.tok("S")])
    b.dma("gpsimd", p_rs[:, :], rsacc[:, :])
    ph1x_done = join_all()

    tok_wz = b.dma("sync", r2(winz_sb, 8, DI),
                   winz_d.rearrange("(k p) f -> p k f", p=128), waits=ph1x_done)
    u2_tok = [None, None]
    u2_free = [None, None]

    def ph1z_load(k):
        slot = k % 2
        u2_tok[slot] = b.dma("sync", r2(u2[slot], 8, T1Z),
                             u_v[:, :, k * T1Z:(k + 1) * T1Z],
                             waits=[u2_free[slot], tok_wz])

    ph1z_load(0)
    rs_tok = b.dma("sync",
                   rs2[0:1, :].rearrange("o (a t) -> o a t", a=2),
                   p_rs[0:1, :].rearrange("o (a t) -> o a t", a=2)[:, :, 0:T1Z],
                   waits=ph1x_done)
    rs_free = None
    spill_s = None
    mm_free = None

    for k in range(N1Z):
        slot = k % 2
        if k + 1 < N1Z:
            ph1z_load(k + 1)
        ut = r2(u2[slot], 8, T1Z)
        w = [u2_tok[slot], rs_tok, rs_free] + (ph1x_done if k == 0 else [])
        temit(lambda e: e.matmul(bank(3, T1Z), ones_row[0:1, :], rs2[0:1, 0:T1Z],
                                 start=True, stop=True), waits=w)
        tok_rep = temit(lambda e: e.matmul(bank(4, T1Z), ones_row[0:1, :],
                                           rs2[0:1, T1Z:2 * T1Z],
                                           start=True, stop=True))
        for c in range(8):
            vemit(lambda e, c=c, ut=ut: e.tensor_mul(xh1z[:, c * T1Z:(c + 1) * T1Z],
                                                     ut[:, c, :], bank(3, T1Z)),
                  waits=[tok_rep] if c == 0 else ())
            vemit(lambda e, c=c: e.tensor_sub(xh1z[:, c * T1Z:(c + 1) * T1Z],
                                              xh1z[:, c * T1Z:(c + 1) * T1Z],
                                              bank(4, T1Z)))
        tok_xh = b.tok("V")
        u2_free[slot] = tok_xh
        rs_free = tok_xh
        if k + 1 < N1Z:
            rs_tok = b.dma("sync",
                           rs2[0:1, :].rearrange("o (a t) -> o a t", a=2),
                           p_rs[0:1, :].rearrange("o (a t) -> o a t", a=2)
                           [:, :, (k + 1) * T1Z:(k + 2) * T1Z],
                           waits=[tok_rep])

        zw = r2(winz_sb, 8, DI)
        ev = {}
        for oc in range(NDC):
            bk = oc % 2
            w = [tok_xh, ev.get(oc - 2)]
            if oc < 2:
                w.append(mm_free)
            for kk in range(8):
                temit(lambda e, oc=oc, kk=kk, bk=bk: e.matmul(
                    bank(bk, T1Z), zw[:, kk, oc * 128:(oc + 1) * 128],
                    xh1z[:, kk * T1Z:(kk + 1) * T1Z],
                    start=(kk == 0), stop=(kk == 7)),
                    waits=w if kk == 0 else ())
            tmm = b.tok("T")
            semit(lambda e, oc=oc, bk=bk: e.activation(
                ssb[:, oc * T1Z:(oc + 1) * T1Z], bank(bk, T1Z), ACT.Silu,
                bias=binz_sb[:, oc:oc + 1]), waits=[tmm])
            ev[oc] = b.tok("S")
        mm_free = b.tok("S")
        spill_s = b.dma("gpsimd", cpt(p_s[:, :, k * T1Z:(k + 1) * T1Z]),
                        r2(ssb, NDC, T1Z), waits=[mm_free, spill_s])

    def barrier(waits):
        def fn(e):
            return e.collective_compute("AllReduce", ALU.add, replica_groups=PAIRS,
                                        ins=[bar_i[:, :]], outs=[bar_o[:, :]])
        return b.emit("gpsimd", fn, waits=waits, inc="BAR")

    tok_bar1 = barrier(join_all())

    pj = join_all() + [tok_bar1]
    tok_sel = b.dma("sync", sel_sb[0:32, :], sel_d[:, :], waits=pj)
    selv = sel_sb[0:32, :].rearrange("p (s f) -> p s f", s=2 * NMY)

    dl_tok = {}
    dl_free = {}
    next_load = [0]

    def ph2_issue_loads(upto):
        while next_load[0] <= min(upto, N2 * NDC - 1):
            gi = next_load[0]
            next_load[0] += 1
            jj, dcc = divmod(gi, NDC)
            slot = gi % 4
            half, tof = jj // 8, (jj % 8) * T2
            w = [tok_bar1]
            if slot in dl_free:
                w.extend(dl_free[slot])
            t1 = b.dma("sync", dlt2[slot][:, :], sh_delta[half, dcc, :, tof:tof + T2],
                       waits=w)
            t2 = b.dma("sync", dxt2[slot][:, :], sh_deltax[half, dcc, :, tof:tof + T2])
            dl_tok[gi] = (t1, t2)

    ph2_issue_loads(3)

    bct_tok = None
    bct_free = None
    da_slot = [0]
    da_free = [None, None]
    spill_y = None
    tile_v_end = None

    for j in range(N2):
        half, tof = j // 8, (j % 8) * T2
        bct_tok = b.dma("sync", bct[0:32, :], sh_bc[half, :, tof:tof + T2],
                        waits=[tok_bar1, bct_free])
        evs = []
        for n in range(NMY):
            for which, dst in ((0, brep), (1, crep)):
                bk = (2 * n + which) % 2
                w = [tok_sel, bct_tok]
                if len(evs) >= 2:
                    w.append(evs[-2])
                if len(evs) < 2 and tile_v_end is not None:
                    w.append(tile_v_end)
                soff = which * NMY
                temit(lambda e, n=n, soff=soff, bk=bk: e.matmul(
                    bank(bk, T2), selv[:, soff + n, :], bct[0:32, :],
                    start=True, stop=True), waits=w)
                tmm = b.tok("T")
                semit(lambda e, n=n, dst=dst, bk=bk: e.activation(
                    dst[:, n * T2:(n + 1) * T2], bank(bk, T2), ACT.Copy),
                    waits=[tmm])
                evs.append(b.tok("S"))
        tok_bcr = b.tok("S")
        bct_free = b.tok("T")

        for dc in range(NDC):
            gi = j * NDC + dc
            ph2_issue_loads(gi + 3)
            slot = gi % 4
            ltok = dl_tok[gi]

            das = da_slot[0]
            da_slot[0] = 1 - da_slot[0]
            dav = da2[das]
            for n in range(NMY):
                w = [ltok[0], tok_amy]
                if n == 0:
                    w.append(da_free[das])
                    if dc == 0:
                        w.append(tok_bcr)
                semit(lambda e, n=n, dc=dc, dav=dav, slot=slot: e.activation(
                    dav[:, n * T2:(n + 1) * T2], dlt2[slot][:, :], ACT.Exp,
                    scale=amy_sb[:, dc * NMY + n:dc * NMY + n + 1]),
                    waits=w if n == 0 else ())
            tok_da = b.tok("S")

            tok_b = vemit(lambda e, slot=slot: e.scalar_tensor_tensor(
                r2(b2, NMY, T2),
                dxt2[slot][:, :].rearrange("p (a t) -> p a t", a=1)
                .broadcast_to((128, NMY, T2)),
                1.0, r2(brep, NMY, T2), ALU.mult, ALU.mult),
                waits=[ltok[1], tok_bcr, tok_da])
            dl_free[slot] = (tok_da, tok_b)

            for n in range(NMY):
                vemit(lambda e, n=n, dav=dav, dc=dc: e.tensor_tensor_scan(
                    h2[:, n * T2:(n + 1) * T2], dav[:, n * T2:(n + 1) * T2],
                    b2[:, n * T2:(n + 1) * T2],
                    car_sb[:, dc * NMY + n:dc * NMY + n + 1],
                    ALU.mult, ALU.add))
            for n in range(NMY):
                vemit(lambda e, n=n, dc=dc: e.tensor_copy(
                    car_sb[:, dc * NMY + n:dc * NMY + n + 1],
                    h2[:, n * T2 + T2 - 1:n * T2 + T2]))
            da_free[das] = b.tok("V")
            vemit(lambda e: e.tensor_tensor(
                hc2[:, :].rearrange("p (t n) -> p n t", n=NMY),
                r2(h2, NMY, T2), r2(crep, NMY, T2), ALU.mult))
            w = [spill_y] if (dc == 0 and spill_y is not None) else ()
            vemit(lambda e, dc=dc: e.tensor_reduce(
                y2[:, dc * T2:(dc + 1) * T2],
                hc2[:, :].rearrange("p (t n) -> p t n", n=NMY), AX.X, ALU.add),
                waits=w)
        tile_v_end = b.tok("V")

        def fny(e, j=j, half=half, tof=tof):
            with e.If_eq(b.regs["gpsimd"], half):
                e.dma_start(out=cpt(p_yp[:, :, tof:tof + T2]),
                            in_=r2(y2, NDC, T2)).then_inc(b.sems["GD"], 16)
            with e.Else():
                e.dma_start(out=cpt(sh_yp[half, :, :, tof:tof + T2]),
                            in_=r2(y2, NDC, T2)).then_inc(b.sems["GD"], 16)
            return None
        spill_y = b.emit("gpsimd", fny, waits=[tile_v_end, spill_y], inc="GD", amt=16)

    tok_bar2 = barrier(join_all())

    pj = join_all() + [tok_bar2]
    tok_wo3 = b.dma("sync", r2(wout_sb, NDC, D),
                    wout_d.rearrange("(k p) f -> p k f", p=128), waits=pj)
    fr = {}
    for i in range(N3):
        tof = i * T3
        tok_yp1 = b.dma("sync", r2(big1, NDC, T3), cpt(p_yp[:, :, tof:tof + T3]),
                        waits=[tok_bar2, fr.get("big1")])

        def fnyp(e, tof=tof):
            with e.If_eq(b.regs["sync"], 0):
                e.dma_start(out=r2(big2, NDC, T3),
                            in_=cpt(sh_yp[0, :, :, tof:tof + T3])).then_inc(
                                b.sems["LD"], 16)
            with e.Else():
                e.dma_start(out=r2(big2, NDC, T3),
                            in_=cpt(sh_yp[1, :, :, tof:tof + T3])).then_inc(
                                b.sems["LD"], 16)
            return None
        tok_yp2 = b.emit("sync", fnyp, waits=[tok_bar2, fr.get("big2")],
                         inc="LD", amt=16)
        tok_y = vemit(lambda e: e.tensor_add(yg3[:, :], big1[:, :], big2[:, :]),
                      waits=[tok_yp1, tok_yp2, fr.get("yg")])
        tok_g1 = b.dma("sync", r2(big2, NDC, T3), cpt(p_g1[:, :, tof:tof + T3]),
                       waits=[tok_y])
        tok_yg1 = vemit(lambda e: e.tensor_add(yg3[:, :], yg3[:, :], big2[:, :]),
                        waits=[tok_g1])
        tok_s = b.dma("sync", r2(big2, NDC, T3), cpt(p_s[:, :, tof:tof + T3]),
                      waits=[tok_yg1])
        tok_yg = vemit(lambda e: e.tensor_mul(yg3[:, :], yg3[:, :], big2[:, :]),
                       waits=[tok_s])
        fr["big2"] = tok_yg
        if debug:
            b.dma("gpsimd", cpt(p_dy[:, :, tof:tof + T3]), r2(big2, NDC, T3),
                  waits=[tok_yg])
            b.dma("gpsimd", cpt(p_dyg[:, :, tof:tof + T3]), r2(yg3, NDC, T3),
                  waits=[tok_yg])
        fr["big1"] = tok_y
        tok_u = b.dma("sync", r2(u3, NDM, T3), u_v[:, :, tof:tof + T3],
                      waits=[tok_y, fr.get("xh3")])

        wov = r2(wout_sb, NDC, D)
        xev = {}
        for do in range(NDM):
            bk = do % 2
            w = [tok_yg, tok_wo3, xev.get(do - 2)]
            if do < 2:
                w.append(fr.get("mm3"))
            for kk in range(NDC):
                temit(lambda e, do=do, kk=kk, bk=bk: e.matmul(
                    bank(bk, T3), wov[:, kk, do * 128:(do + 1) * 128],
                    yg3[:, kk * T3:(kk + 1) * T3],
                    start=(kk == 0), stop=(kk == NDC - 1)),
                    waits=w if kk == 0 else ())
            tmm = b.tok("T")
            vemit(lambda e, do=do, bk=bk: e.scalar_tensor_tensor(
                xr3[:, do * T3:(do + 1) * T3], bank(bk, T3),
                bout_sb[:, do:do + 1], u3[:, do * T3:(do + 1) * T3],
                ALU.add, ALU.add),
                waits=[tmm, tok_u] + ([fr.get("xr")] if do == 0 else []))
            xev[do] = b.tok("V")
        tok_xr = b.tok("V")
        fr["mm3"] = tok_xr

        sq_t = None
        for c in range(NDM):
            sph = c % 2
            w = [tok_xr] + ([fr.get("rp")] if c == 0 else [])
            if sq_t is not None and c >= 2:
                w.append(sq_t)
            stok = semit(lambda e, c=c, sph=sph: e.activation(
                sq3a[:, sph * T3:(sph + 1) * T3], xr3[:, c * T3:(c + 1) * T3],
                ACT.Square), waits=w)
            temit(lambda e, c=c: e.matmul(bank(2)[0:1, 0:T3], ones_col[:, :],
                                          xr3[:, c * T3:(c + 1) * T3],
                                          start=(c == 0), stop=(c == NDM - 1)),
                  waits=[tok_xr] if c == 0 else ())
            temit(lambda e, c=c, sph=sph: e.matmul(
                bank(5)[0:1, 0:T3], ones_col[:, :], sq3a[:, sph * T3:(sph + 1) * T3],
                start=(c == 0), stop=(c == NDM - 1)), waits=[stok])
            sq_t = b.tok("T")
        tok_st = b.tok("T")
        rm = rp3[0:1, 0:T3]
        rv = rp3[0:1, T3:2 * T3]
        rq = rp3[0:1, 2 * T3:3 * T3]
        w = [tok_st] + ([fr.get("rp")] if "rp" in fr else [])
        vemit(lambda e: e.tensor_scalar_mul(rm, bank(2)[0:1, 0:T3], 1.0 / D), waits=w)
        vemit(lambda e: e.tensor_scalar_mul(rv, bank(5)[0:1, 0:T3], 1.0 / D))
        vemit(lambda e: e.tensor_mul(rq, rm, rm))
        tok_v3 = vemit(lambda e: e.tensor_sub(rv, rv, rq))
        semit(lambda e: e.activation(rv, rv, ACT.Ln, bias=eps_sb[0:1, :]),
              waits=[tok_v3])
        tok_rstd3 = semit(lambda e: e.activation(rv, rv, ACT.Exp, scale=-0.5))
        tok_mr3 = vemit(lambda e: e.tensor_mul(rm, rm, rv), waits=[tok_rstd3])
        temit(lambda e: e.matmul(bank(3, T3), ones_row[0:1, :], rv,
                                 start=True, stop=True), waits=[tok_mr3])
        tok_rep3 = temit(lambda e: e.matmul(bank(4, T3), ones_row[0:1, :], rm,
                                            start=True, stop=True))
        fr["rp"] = tok_rep3
        for c in range(NDM):
            vemit(lambda e, c=c: e.tensor_mul(xh3[:, c * T3:(c + 1) * T3],
                                              xr3[:, c * T3:(c + 1) * T3],
                                              bank(3, T3)),
                  waits=[tok_rep3] + ([fr.get("xh3w")] if c == 0 else []))
            vemit(lambda e, c=c: e.tensor_sub(xh3[:, c * T3:(c + 1) * T3],
                                              xh3[:, c * T3:(c + 1) * T3],
                                              bank(4, T3)))
        tok_xh3 = b.tok("V")

        tok_spxr = b.dma("gpsimd", cpt(p_xr2[:, :, tof:tof + T3]),
                         r2(xr3, NDM, T3), waits=[tok_xh3, fr.get("spxr")])
        tok_spxh = b.dma("gpsimd", cpt(p_xh2[:, :, tof:tof + T3]),
                         r2(xh3, NDM, T3), waits=[fr.get("spxh")])
        fr["spxr"] = tok_spxr
        fr["spxh"] = tok_spxh
        fr["xr"] = tok_spxr
        fr["yg"] = tok_spxr
        fr["xh3"] = tok_spxh
        fr["xh3w"] = tok_spxh
        fr["big1"] = tok_spxh
        fr["big2"] = tok_spxr

    ph3a_done = join_all()

    b.dma("sync", r2(wg_sb, 8, NE), wg_d.rearrange("(k p) f -> p k f", p=128),
          waits=ph3a_done)
    b.dma("sync", r2(acat_sb, 8, NE * NR),
          acat_d.rearrange("(k p) f -> p k f", p=128))
    tok_wcst = b.tok("LD")

    xh4_tok = [None, None]
    ff = {}

    def ld3b(i):
        slot = i % 2
        xh4_tok[slot] = b.dma("sync", r2(xh4[slot], NDM, T3),
                              cpt(p_xh2[:, :, i * T3:(i + 1) * T3]),
                              waits=[tok_wcst, ff.get(("xh4", slot))] +
                              (ph3a_done if i == 0 else []))

    ld3b(0)
    wic_tok = [None, None]
    woc_tok = [None, None]
    bcc_tok = [None, None]
    wgv = r2(wg_sb, 8, NE)
    acv = r2(acat_sb, 8, NE * NR)
    xr4_tok = None

    for i in range(N3):
        slot = i % 2
        if i + 1 < N3:
            ld3b(i + 1)
        xhv = xh4[slot]
        tof = i * T3

        for kk in range(NDM):
            temit(lambda e, kk=kk, xhv=xhv: e.matmul(
                bank(5, T3)[0:NE, :], wgv[:, kk, :], xhv[:, kk * T3:(kk + 1) * T3],
                start=(kk == 0), stop=(kk == NDM - 1)),
                waits=([xh4_tok[slot], tok_wcst, ff.get("b5")] if kk == 0 else ()))
        tok_lg = b.tok("T")
        tok_lev4 = semit(lambda e: e.activation(lgt4[0:NE, :], bank(5, T3)[0:NE, :],
                                                ACT.Copy),
                         waits=[tok_lg, ff.get("lgt4")])
        tok_lgr = b.dma("sync", lgr0.rearrange("o (n t) -> o n t", n=NE),
                        lgt4[0:NE, :], waits=[tok_lev4, ff.get("lgr")])
        ff["lgt4"] = tok_lgr
        tok_bgadd = vemit(lambda e: e.tensor_tensor(
            lgr0.rearrange("o (n t) -> o n t", n=NE),
            lgr0.rearrange("o (n t) -> o n t", n=NE),
            bg_sb[0:1, :].rearrange("o (n a) -> o n a", n=NE)
            .broadcast_to((1, NE, T3)), ALU.add),
            waits=[tok_lgr, ff.get("el")])
        tok_el = semit(lambda e: e.activation(expl0[:, :], lgr0[:, :], ACT.Exp),
                       waits=[tok_bgadd])

        def E(e_i):
            return expl0[:, e_i * T3:(e_i + 1) * T3]

        ta = ts0[:, 0:T3]
        tb = ts0[:, T3:2 * T3]
        tsum = ts0[:, 2 * T3:3 * T3]
        trinv = ts0[:, 3 * T3:4 * T3]
        w = [tok_el, ff.get("tsc")]
        vemit(lambda e: e.tensor_add(tsum, E(0), E(1)), waits=w)
        vemit(lambda e: e.tensor_add(ta, E(2), E(3)))
        vemit(lambda e: e.tensor_add(tsum, tsum, ta))
        vemit(lambda e: e.reciprocal(trinv, tsum))
        W4 = [w40[:, e_i * T3:(e_i + 1) * T3] for e_i in range(NE)]
        w = [ff.get("w40")]
        vemit(lambda e: e.tensor_tensor(W4[0], E(0), E(1), ALU.max), waits=w)
        vemit(lambda e: e.tensor_tensor(W4[1], E(0), E(1), ALU.min))
        vemit(lambda e: e.tensor_tensor(W4[2], E(2), E(3), ALU.max))
        vemit(lambda e: e.tensor_tensor(W4[3], E(2), E(3), ALU.min))
        vemit(lambda e: e.tensor_tensor(ta, W4[0], W4[2], ALU.min))
        vemit(lambda e: e.tensor_tensor(tb, W4[1], W4[3], ALU.max))
        tok_m2 = vemit(lambda e: e.tensor_tensor(ta, ta, tb, ALU.max))
        vemit(lambda e: e.tensor_tensor(
            w40[:, :].rearrange("o (n t) -> o n t", n=NE),
            expl0[:, :].rearrange("o (n t) -> o n t", n=NE),
            ta.rearrange("o (a t) -> o a t", a=1).broadcast_to((1, NE, T3)),
            ALU.is_ge))
        vemit(lambda e: e.tensor_mul(w40[:, :], w40[:, :], expl0[:, :]))
        tok_w4 = vemit(lambda e: e.tensor_tensor(
            w40[:, :].rearrange("o (n t) -> o n t", n=NE),
            w40[:, :].rearrange("o (n t) -> o n t", n=NE),
            trinv.rearrange("o (a t) -> o a t", a=1).broadcast_to((1, NE, T3)),
            ALU.mult))
        ff["el"] = tok_w4
        ff["tsc"] = tok_w4
        ff["lgr"] = tok_w4
        ff["bank7"] = tok_w4

        wev = []
        for e_i in range(NE):
            bk = e_i % 2
            w = [tok_w4]
            if len(wev) >= 2:
                w.append(wev[-2])
            elif ff.get("b01") is not None:
                w.append(ff.get("b01"))
            temit(lambda e, e_i=e_i, bk=bk: e.matmul(
                bank(bk, T3), ones_row[0:1, :], w40[:, e_i * T3:(e_i + 1) * T3],
                start=True, stop=True), waits=w)
            tmm = b.tok("T")
            semit(lambda e, e_i=e_i, bk=bk: e.activation(
                wrep[:, e_i * T3:(e_i + 1) * T3], bank(bk, T3), ACT.Copy),
                waits=[tmm])
            wev.append(b.tok("S"))
        tok_wrep = b.tok("S")
        ff["w40"] = tok_wrep

        for kk in range(NDM):
            temit(lambda e, kk=kk, xhv=xhv: e.matmul(
                bank(5, T3)[0:NE * NR, :], acv[:, kk, :],
                xhv[:, kk * T3:(kk + 1) * T3],
                start=(kk == 0), stop=(kk == NDM - 1)),
                waits=([tok_el] if kk == 0 else ()))
        tok_llm = b.tok("T")
        ff["b5"] = tok_llm
        tok_ll = semit(lambda e: e.activation(llsb16[0:NE * NR, :],
                                              bank(5, T3)[0:NE * NR, :], ACT.Copy),
                       waits=[tok_llm, ff.get("llsb")])

        gdone = []
        hp_done = []
        lev_done = []
        for fc in range(NFC):
            ws = fc % 2
            bb = 2 if fc % 2 == 0 else 7
            wic_tok[ws] = b.dma("sync", r2(wic[ws], 8, 128),
                                wi_d[:, fc * 128:(fc + 1) * 128]
                                .rearrange("(k p) f -> p k f", p=128),
                                waits=[wic_tok[ws],
                                       gdone[-2] if len(gdone) >= 2 else None] +
                                ([tok_wcst] if fc < 2 else []))
            bcc_tok[ws] = b.dma("sync", bcatc[ws][0:16, :],
                                bcat_d[:, fc * NE * 128:(fc + 1) * NE * 128],
                                waits=[bcc_tok[ws],
                                       gdone[-2] if len(gdone) >= 2 else None])
            for kk in range(NDM):
                w = ()
                if kk == 0:
                    w = [wic_tok[ws]]
                    if len(hp_done) >= 2:
                        w.append(hp_done[-2])
                    if fc == 0:
                        w.append(ff.get("bank7"))
                temit(lambda e, kk=kk, ws=ws, xhv=xhv, bb=bb: e.matmul(
                    bank(bb, T3), wic[ws][:, kk * 128:(kk + 1) * 128],
                    xhv[:, kk * T3:(kk + 1) * T3],
                    start=(kk == 0), stop=(kk == NDM - 1)), waits=w)
            tok_base = b.tok("T")
            for e_i in range(NE):
                w = ()
                if e_i == 0:
                    w = [bcc_tok[ws], tok_ll, tok_w4]
                    if lev_done:
                        w.append(lev_done[-1])
                temit(lambda e, e_i=e_i, ws=ws: e.matmul(
                    bank(3 + e_i, T3), bcatc[ws][0:16, e_i * 128:(e_i + 1) * 128],
                    llsb16[0:16, :], start=True, stop=True), waits=w)
            tok_lora = b.tok("T")
            semit(lambda e, ws=ws: e.activation(
                lor4[ws][:, :], ps[:, 3 * 512:3 * 512 + NE * T3], ACT.Copy),
                waits=[tok_lora] + ([gdone[-2]] if len(gdone) >= 2 else []))
            tok_lev = b.tok("S")
            lev_done.append(tok_lev)
            vemit(lambda e, ws=ws, bb=bb: e.scalar_tensor_tensor(
                r2(hp4, NE, T3),
                bank(bb, T3).rearrange("p (a t) -> p a t", a=1)
                .broadcast_to((128, NE, T3)),
                1.0, r2(lor4[ws], NE, T3), ALU.mult, ALU.add),
                waits=[tok_base, tok_lev] +
                ([ff.get("outd")] if fc == 0 else []))
            tok_hp = b.tok("V")
            hp_done.append(tok_hp)
            vemit(lambda e: e.tensor_scalar_max(hr4[:, :], hp4[:, :], 0.0))
            vemit(lambda e: e.tensor_tensor(
                hw4[:, :].rearrange("p (t n) -> p n t", n=NE),
                r2(hr4, NE, T3), r2(wrep, NE, T3), ALU.mult),
                waits=[tok_wrep])
            vemit(lambda e, fc=fc: e.tensor_reduce(
                g4[:, fc * T3:(fc + 1) * T3],
                hw4[:, :].rearrange("p (t n) -> p t n", n=NE), AX.X, ALU.add))
            gdone.append(b.tok("V"))
        tok_g = b.tok("V")
        ff["llsb"] = tok_g

        if debug:
            b.dma("gpsimd", cpt(p_dg[:, :, tof:tof + T3]), r2(g4, NFC, T3),
                  waits=[tok_g])
            b.dma("gpsimd", p_dw[0:1, :].rearrange("o (n t) -> o n t", n=NE)
                  [:, :, tof:tof + T3],
                  w40.rearrange("o (n t) -> o n t", n=NE), waits=[tok_g])
            b.dma("gpsimd", p_dll[:, tof:tof + T3], llsb16[0:16, :], waits=[tok_g])
            b.dma("gpsimd", cpt(p_dbase[:, :, tof:tof + T3]), r2(hp4, NE, T3),
                  waits=[tok_g])
        xr4_tok = b.dma("sync", r2(xr4, NDM, T3), cpt(p_xr2[:, :, tof:tof + T3]),
                        waits=[ff.get("xr4")])

        wo_mm = {}
        for kk in range(NFC):
            ws = kk % 2
            woc_tok[ws] = b.dma("sync", woc[ws][:, :],
                                wo_d[kk * 128:(kk + 1) * 128, :],
                                waits=[woc_tok[ws], tok_g, wo_mm.get(kk - 2)])
            for do in range(NDM):
                w = ()
                if do == 0:
                    w = [woc_tok[ws], tok_g]
                    if kk == 0:
                        w.append(ff.get("out"))
                temit(lambda e, kk=kk, do=do, ws=ws: e.matmul(
                    bank(do, T3), woc[ws][:, do * 128:(do + 1) * 128],
                    g4[:, kk * T3:(kk + 1) * T3],
                    start=(kk == 0), stop=(kk == NFC - 1)), waits=w)
            wo_mm[kk] = b.tok("T")
        tok_wo = b.tok("T")
        for do in range(NDM):
            vemit(lambda e, do=do: e.tensor_add(
                out4[:, do * T3:(do + 1) * T3], bank(do, T3),
                xr4[:, do * T3:(do + 1) * T3]),
                waits=[tok_wo, xr4_tok, ff.get("outd")] if do == 0 else ())
        tok_out = b.tok("V")
        ff[("xh4", slot)] = tok_g
        ff["xr4"] = tok_out
        ff["b5"] = tok_out
        ff["b01"] = tok_out
        ff["out"] = b.dma("sync", out_v[:, :, tof:tof + T3], r2(out4, NDM, T3),
                          waits=[tok_out])
        ff["outd"] = ff["out"]

    if debug:
        allj = join_all()
        dbg_car = nc.dram_tensor("d_car", [128, NDC * NMY], F32,
                                 kind="ExternalOutput").ap()
        b.dma("sync", dbg_car[:, :], car_sb[:, :], waits=allj)
        dbg = {
            "d_delta": (sh_delta, [2, NDC, 128, TH]),
            "d_deltax": (sh_deltax, [2, NDC, 128, TH]),
            "d_bc": (sh_bc, [2, 32, TH]),
            "d_yp": (p_yp, [NDC, 128, TH]),
            "d_shyp": (sh_yp, [2, NDC, 128, TH]),
            "d_s": (p_s, [NDC, 128, TH]),
            "d_g1": (p_g1, [NDC, 128, TH]),
            "d_rs": (p_rs, [1, 2 * TH]),
            "d_xr2": (p_xr2, [NDM, 128, TH]),
            "d_dy": (p_dy, [NDC, 128, TH]),
            "d_dg": (p_dg, [NFC, 128, TH]),
            "d_dw": (p_dw, [1, NE * TH]),
            "d_dll": (p_dll, [16, TH]),
            "d_dbase": (p_dbase, [NE, 128, TH]),
            "d_dyg": (p_dyg, [NDC, 128, TH]),
            "d_xh2": (p_xh2, [NDM, 128, TH]),
        }
        for nm, (src_ap, shp) in dbg.items():
            od = nc.dram_tensor(nm, shp, F32, kind="ExternalOutput").ap()
            b.dma("sync", od[tuple(slice(None) for _ in shp)], src_ap[
                tuple(slice(None) for _ in shp)], waits=allj)

    final_ld = b.tok("LD")
    final_gd = b.tok("GD")
    b.emit("sync", lambda e: None, waits=[final_ld], inc=None)
    b.emit("gpsimd", lambda e: None, waits=[final_gd], inc=None)

    with (
        nc.semaphore("sLD") as s_ld,
        nc.semaphore("sGD") as s_gd,
        nc.semaphore("sT") as s_t,
        nc.semaphore("sS") as s_s,
        nc.semaphore("sV") as s_v,
        nc.semaphore("sBAR") as s_bar,
        nc.Block() as block,
    ):
        b.sems.update({"LD": s_ld, "GD": s_gd, "T": s_t, "S": s_s, "V": s_v,
                       "BAR": s_bar})
        b.regs.update({"sync": nc.sync.alloc_register("par_sync"),
                       "gpsimd": nc.gpsimd.alloc_register("par_gp")})

        def mk(eng):
            def body(e):
                for fn, waits, inc, amt in b.q[eng]:
                    for (sn, val) in waits:
                        e.wait_ge(b.sems[sn], val)
                    ins = fn(e)
                    if inc is not None and ins is not None:
                        ins.then_inc(b.sems[inc], amt)
            return body

        block.sync(mk("sync"))
        block.scalar(mk("scalar"))
        block.vector(mk("vector"))
        block.tensor(mk("tensor"))
        block.gpsimd(mk("gpsimd"))

    return nc


_CACHE = {}


def _get_nc(debug=False):
    key = "ncd" if debug else "nc"
    if key not in _CACHE:
        _CACHE[key] = build_program(debug)
    return _CACHE[key]


def _prep_inputs(inputs):
    def c(x):
        return np.ascontiguousarray(np.asarray(x), dtype=np.float32)

    u = c(inputs["L0_attn_u"])
    W_in = c(inputs["L0_attn_W_in"])
    b_in = c(inputs["L0_attn_b_in"])
    W_x = c(inputs["L0_attn_W_x"])
    b_x = c(inputs["L0_attn_b_x"])
    W_dt = c(inputs["L0_attn_W_dt"])
    b_dt = c(inputs["L0_attn_b_dt"])
    A_log = c(inputs["L0_attn_A_log"])
    Dv = c(inputs["L0_attn_D"])
    W_out = c(inputs["L0_attn_W_out"])
    b_out = c(inputs["L0_attn_b_out"])
    Wg = c(inputs["L0_ffn_Wg"])
    bg = c(inputs["L0_ffn_bg"])
    Wi = c(inputs["L0_ffn_Wi"])
    Wo = c(inputs["L0_ffn_Wo"])
    Af = c(inputs["L0_ffn_A"])
    Bf = c(inputs["L0_ffn_B"])

    bcat = np.zeros((16, NFC, NE, 128), np.float32)
    for e_i in range(NE):
        bcat[e_i * NR:(e_i + 1) * NR, :, e_i, :] = (
            Bf[e_i].reshape(NFC, 128, NR).transpose(2, 0, 1))
    bcat = bcat.reshape(16, NFC * NE * 128)

    shared = {
        "winx": np.ascontiguousarray(W_in[:DI].T),
        "winz": np.ascontiguousarray(W_in[DI:].T),
        "binx": np.ascontiguousarray(b_in[:DI].reshape(NDC, 128).T),
        "binz": np.ascontiguousarray(b_in[DI:].reshape(NDC, 128).T),
        "wx": np.ascontiguousarray(W_x.T),
        "bx": np.ascontiguousarray(b_x.reshape(96, 1)),
        "wdt": np.ascontiguousarray(W_dt.T),
        "bdt": np.ascontiguousarray(b_dt.reshape(NDC, 128).T),
        "dvec": np.ascontiguousarray(Dv.reshape(NDC, 128).T),
        "wout": np.ascontiguousarray(W_out.T),
        "bout": np.ascontiguousarray(b_out.reshape(NDM, 128).T),
        "wg": np.ascontiguousarray(Wg.T),
        "bg": np.ascontiguousarray(bg.reshape(1, NE)),
        "wi": np.ascontiguousarray(Wi.T),
        "acat": np.ascontiguousarray(Af.reshape(NE * NR, D).T),
        "bcat": np.ascontiguousarray(bcat),
        "wo": np.ascontiguousarray(Wo.T),
    }

    in_maps = []
    for c_id in range(8):
        bidx, q = c_id // 2, c_id % 2
        m = dict(shared)
        m["u"] = np.ascontiguousarray(u[bidx, q * TH:(q + 1) * TH, :].T)
        m["par"] = np.full((1, 1), q, np.int32)
        ns = list(range(q * NMY, (q + 1) * NMY))
        amy = A_log[:, ns].reshape(NDC, 128, NMY).transpose(1, 0, 2)
        m["amy"] = np.ascontiguousarray(amy.reshape(128, NDC * NMY))
        sel = np.zeros((32, 2 * NMY * 128), np.float32)
        for jn, gn in enumerate(ns):
            sel[gn, jn * 128:(jn + 1) * 128] = 1.0
            sel[16 + gn, (NMY + jn) * 128:(NMY + jn + 1) * 128] = 1.0
        m["sel"] = sel
        in_maps.append(m)
    return in_maps


def kernel(**inputs):
    nc = _get_nc()
    in_maps = _prep_inputs(inputs)
    res = run_bass_kernel_spmd(nc, in_maps, core_ids=list(range(8)))
    out = np.empty((4, TF, D), np.float32)
    for c_id in range(8):
        bidx, q = c_id // 2, c_id % 2
        out[bidx, q * TH:(q + 1) * TH, :] = res.results[c_id]["out"].T
    return out
